# version 02
# baseline (speedup 1.0000x reference)
import numpy as np
import ml_dtypes
from contextlib import ExitStack

import os
import concourse.bass as bass
import concourse.mybir as mybir
from concourse.bass_utils import run_bass_kernel_spmd

F32 = mybir.dt.float32
BF16 = mybir.dt.bfloat16
I8 = mybir.dt.int8
U8 = mybir.dt.uint8
AF = mybir.ActivationFunctionType
ALU = mybir.AluOpType
AX = mybir.AxisListType

N_CORES = 8
KPH = int(os.environ.get("KPH", "99"))
KNH = int(os.environ.get("KNH", "16"))
KSV = int(os.environ.get("KSV", "1"))
KDUMP = int(os.environ.get("KDUMP", "0"))
KSERIAL = int(os.environ.get("KSERIAL", "0"))
D = 2048
T = 2048
NT = 16
NF = 16
HD = 128
H0 = 16
CH = 64
NCH = 32
EPS = 1e-5
NEXP = 16
TOPK = 1024
BISECT_ITERS = 38

ENGINES = ("sync", "gpsimd", "vector", "scalar", "tensor")


class Rec:

    def __init__(self, nc):
        self.nc = nc
        self.q = {e: [] for e in ENGINES}
        self.last = {e: None for e in ENGINES}
        self.dirty = {e: False for e in ENGINES}
        self.sem = {}
        self.cnt = {}

    def add_sem(self, name, handle):
        self.sem[name] = handle
        self.cnt[name] = 0

    def op(self, e, fn, inc=None, inc_n=1):
        if inc is not None:
            sem = self.sem[inc]
            def cl(eng, fn=fn, sem=sem, n=inc_n, e=e):
                inst = fn(eng)
                inst.then_inc(sem, n)
                self.last[e] = inst
        else:
            def cl(eng, fn=fn, e=e):
                inst = fn(eng)
                if inst is not None:
                    self.last[e] = inst
        self.q[e].append(cl)
        own = {"tensor": "T", "vector": "V", "scalar": "A", "gpsimd": "G"}.get(e)
        self.dirty[e] = not (inc is not None and inc == own)
        if inc is not None:
            self.cnt[inc] += inc_n
            self.wait(e, inc, self.cnt[inc])
            return self.cnt[inc]
        return None

    def wait(self, e, sem_name, val):
        if val <= 0:
            return
        sem = self.sem[sem_name]
        self.q[e].append(lambda eng, sem=sem, val=val: eng.wait_ge(sem, val))

    def dma(self, queue, out, in_, **kw):
        semname = {"sync": "SD", "gpsimd": "GD", "scalar": "AD"}[queue]
        r = self.op(queue, lambda eng: eng.dma_start(out=out, in_=in_, **kw),
                    inc=semname, inc_n=16)
        self.dirty[queue] = False
        self.last[queue] = None
        return r

    def bump(self, e):
        semname = {"tensor": "T", "vector": "V", "scalar": "A", "gpsimd": "G"}[e]
        sem = self.sem[semname]
        def cl(eng, sem=sem, e=e):
            assert self.last[e] is not None, f"bump on {e} with no prior instruction"
            self.last[e].then_inc(sem, 1)
        self.q[e].append(cl)
        self.cnt[semname] += 1
        self.dirty[e] = False
        self.wait(e, semname, self.cnt[semname])
        return self.cnt[semname]

    def barrier(self):
        for e in ("tensor", "vector", "scalar", "gpsimd"):
            if self.dirty[e]:
                self.bump(e)
                self.dirty[e] = False
        targets = [(s, self.cnt[s]) for s in ("SD", "GD", "AD", "T", "V", "A", "G")]
        skip = {"tensor": {"T"}, "vector": {"V"}, "scalar": {"A", "AD"},
                "gpsimd": {"G", "GD"}, "sync": {"SD"}}
        for e in ENGINES:
            for sname, val in targets:
                if sname in skip[e]:
                    continue
                self.wait(e, sname, val)

    def replay(self, block):
        for e in ENGINES:
            ops = self.q[e]
            getattr(block, e)(lambda eng, ops=ops: [f(eng) for f in ops])



def build_consts():
    c = {}
    c["IDENT_BF"] = np.eye(128, dtype=ml_dtypes.bfloat16)
    c["IDENT_F32"] = np.eye(128, dtype=np.float32)
    i = np.arange(CH, dtype=np.float32)
    difft = (i[None, :] - i[:, None])
    c["DIFFT64"] = np.where(difft >= 0, difft, 0.0).astype(np.float32)
    c["MASKT64"] = (difft >= 0).astype(np.float32)
    c["IOTA64_ROW"] = i[None, :].copy()
    c["REV512_ROW"] = (CH - (np.arange(512) % CH)).astype(ml_dtypes.bfloat16)[None, :].copy()
    c["REV64_COL"] = (CH - i)[:, None].copy()
    c["ONES_1x128_BF"] = np.ones((1, 128), dtype=ml_dtypes.bfloat16)
    c["ONES_1x128_F32"] = np.ones((1, 128), dtype=np.float32)
    c["ONES_1x64_F32"] = np.ones((1, 64), dtype=np.float32)
    c["ONES_128x1_BF"] = np.ones((128, 1), dtype=ml_dtypes.bfloat16)
    c["ONES_128x1_F32"] = np.ones((128, 1), dtype=np.float32)
    p = np.arange(128)
    grp = (p[:, None] % 16 == np.arange(16)[None, :]).astype(np.float32)
    c["GRP"] = grp
    c["GRP2"] = np.ascontiguousarray(grp.T)
    dh = HD
    inv = 1.0 / (10000.0 ** (np.arange(0, dh, 2, dtype=np.float32) / dh))
    tpos = np.arange(T, dtype=np.float32)
    freqs = tpos[None, :] * inv[:, None]
    c["COS_FM"] = np.repeat(np.cos(freqs), 2, axis=0).astype(np.float32)
    c["SIN_FM"] = np.repeat(np.sin(freqs), 2, axis=0).astype(np.float32)
    rot = np.zeros((128, 128), dtype=np.float32)
    for m in range(64):
        rot[m + 64, m] = -1.0
    for m in range(64, 128):
        rot[m - 64, m] = 1.0
    c["ROT_F32"] = rot
    jp = np.arange(128)[:, None]
    ip = np.arange(512)[None, :]
    for delta in (128, 256, 384, 512):
        c[f"MASKT_{delta}"] = ((jp - ip) <= (128 - delta)).astype(ml_dtypes.bfloat16)
    return c



def build_program(upto=4):
    nc = bass.Bass("TRN2", target_bir_lowering=False)
    consts_np = build_consts()

    inp = {}
    def din(name, shape, dt):
        inp[name] = nc.dram_tensor(name, shape, dt, kind="ExternalInput").ap()
    din("x_in", [T, D], F32)
    din("L0_attn_Wqkv", [D, 3 * D], F32)
    din("L0_attn_bqkv", [3 * D], F32)
    din("L0_attn_Wout", [D, D], F32)
    din("L0_attn_bout", [D], F32)
    din("L0_attn_slopes", [H0], F32)
    din("L0_ffn_router_w", [D, NEXP], F32)
    din("L0_ffn_router_b", [NEXP], F32)
    din("L0_ffn_expert_w", [D, D], F32)
    din("L0_ffn_expert_b", [D], F32)
    din("L1_attn_Wq", [D, D], F32)
    din("L1_attn_Wk", [512, D], F32)
    din("L1_attn_Wv", [512, D], F32)
    din("L1_attn_Wo", [D, D], F32)
    din("L1_ffn_w_gate", [D, 4 * D], I8)
    din("L1_ffn_w_up", [D, 4 * D], I8)
    din("L1_ffn_w_down", [4 * D, D], I8)
    din("L1_ffn_s_gate", [4 * D], F32)
    din("L1_ffn_s_up", [4 * D], F32)
    din("L1_ffn_s_down", [D], F32)
    for name, arr in consts_np.items():
        dt = BF16 if arr.dtype == ml_dtypes.bfloat16 else F32
        din(name, list(arr.shape), dt)

    out_ap = nc.dram_tensor("out", [T, D], F32, kind="ExternalOutput").ap()

    X = nc.dram_tensor("X", [T, D], F32, kind="Internal").ap()
    WQ_BF = nc.dram_tensor("WQ_BF", [D, D], BF16, kind="Internal").ap()
    WK_BF = nc.dram_tensor("WK_BF", [512, D], BF16, kind="Internal").ap()
    WV_BF = nc.dram_tensor("WV_BF", [512, D], BF16, kind="Internal").ap()
    WO_BF = nc.dram_tensor("WO_BF", [D, D], BF16, kind="Internal").ap()
    CC_IN = nc.dram_tensor("CC_IN", [NEXP, T], F32, kind="Internal")
    CC_OUT = nc.dram_tensor("CC_OUT", [N_CORES * NEXP, T], F32, kind="Internal")

    with ExitStack() as ctx:
        rec = Rec(nc)
        for s in ("SD", "GD", "AD", "T", "V", "A", "G", "CC"):
            rec.add_sem(s, ctx.enter_context(nc.semaphore(s)))

        sb = lambda name, shape, dt: ctx.enter_context(nc.sbuf_tensor(name, shape, dt))
        A1 = sb("A1", [128, 32768], BF16)
        A2 = sb("A2", [128, 32768], BF16)
        x_tile = sb("x_tile", [128, 2048], F32)
        x_new = sb("x_new", [128, 2048], F32)
        xn_tm = sb("xn_tm", [128, 2048], BF16)
        xsq_scr = sb("xsq_scr", [128, 2048], BF16)
        stash = sb("stash", [128, 64], F32) if KDUMP else None
        stats = sb("stats", [128, 32], F32)
        bc_f32 = sb("bc_f32", [128, 2048], F32)
        brow = sb("brow", [1, 2048], F32)
        ident_bf = sb("ident_bf", [128, 128], BF16)
        ident_f32 = sb("ident_f32", [128, 128], F32)
        ones_1x128_bf = sb("ones_1x128_bf", [1, 128], BF16)
        ones_1x128_f32 = sb("ones_1x128_f32", [1, 128], F32)
        ones_1x64_f32 = sb("ones_1x64_f32", [1, 64], F32)
        ones_128x1_bf = sb("ones_128x1_bf", [128, 1], BF16)
        ones_128x1_f32 = sb("ones_128x1_f32", [128, 1], F32)
        eps_col = sb("eps_col", [128, 1], F32)
        rec.op("vector", lambda v: v.memset(eps_col[:, :], EPS))

        def load_const(name, dest):
            shape = consts_np[name].shape
            return rec.dma("sync", dest[tuple(slice(0, s) for s in shape)], inp[name])

        load_const("IDENT_BF", ident_bf)
        load_const("IDENT_F32", ident_f32)
        load_const("ONES_1x128_BF", ones_1x128_bf)
        load_const("ONES_1x128_F32", ones_1x128_f32)
        load_const("ONES_1x64_F32", ones_1x64_f32)
        load_const("ONES_128x1_BF", ones_128x1_bf)
        load_const("ONES_128x1_F32", ones_128x1_f32)

        if upto >= 3:
            rec.dma("gpsimd", WQ_BF, inp["L1_attn_Wq"])
            rec.dma("gpsimd", WK_BF, inp["L1_attn_Wk"])
            rec.dma("gpsimd", WV_BF, inp["L1_attn_Wv"])
            rec.dma("gpsimd", WO_BF, inp["L1_attn_Wo"])

        XN = A1
        XN3 = XN.ap().rearrange("p (c t) -> p c t", c=16)

        ln_count = [0]

        def ln_pass(src_dram):
            last_ct = [0]
            ln_count[0] += 1
            with nc.psum_tensor(f"ln_ps{ln_count[0]}", [128, 2048], BF16) as ln_ps:
                for i in range(NT):
                    c_ld = rec.dma("sync", x_tile[:, :], src_dram[i * 128:(i + 1) * 128, :])
                    rec.wait("vector", "SD", c_ld)
                    if i >= 1:
                        rec.wait("scalar", "T", last_ct[0])
                    sb_ = (i % 2) * 12
                    S0, S1, S2, S3, S4, SM, SV, SL, SR, SN = (sb_ + j for j in range(10))
                    rec.op("vector", lambda v, S0=S0: v.reduce_sum(
                        stats[:, S0:S0 + 1], x_tile[:, :], axis=AX.X), inc="V")
                    cv0 = rec.op("vector", lambda v, S0=S0, SM=SM: v.tensor_scalar_mul(
                        stats[:, SM:SM + 1], stats[:, S0:S0 + 1], 1.0 / 2048), inc="V")
                    rec.wait("scalar", "V", cv0)
                    ca0 = rec.op("scalar", lambda s, S1=S1: s.activation(
                        xsq_scr[:, :], x_tile[:, :], AF.Square,
                        accum_out=stats[:, S1:S1 + 1]), inc="A")
                    rec.wait("vector", "A", ca0)
                    rec.op("vector", lambda v, S1=S1, S2=S2: v.tensor_scalar_mul(
                        stats[:, S2:S2 + 1], stats[:, S1:S1 + 1], 1.0 / 2048), inc="V")
                    rec.op("vector", lambda v, SM=SM, S3=S3: v.tensor_mul(
                        stats[:, S3:S3 + 1], stats[:, SM:SM + 1], stats[:, SM:SM + 1]), inc="V")
                    cvv1 = rec.op("vector", lambda v, S2=S2, S3=S3, SV=SV: v.tensor_sub(
                        stats[:, SV:SV + 1], stats[:, S2:S2 + 1], stats[:, S3:S3 + 1]), inc="V")
                    rec.wait("scalar", "V", cvv1)
                    rec.op("scalar", lambda s, SV=SV, SL=SL: s.activation(
                        stats[:, SL:SL + 1], stats[:, SV:SV + 1], AF.Ln, bias=eps_col[:, :]), inc="A")
                    ca1 = rec.op("scalar", lambda s, SL=SL, SR=SR: s.activation(
                        stats[:, SR:SR + 1], stats[:, SL:SL + 1], AF.Exp, scale=-0.5), inc="A")
                    rec.wait("vector", "A", ca1)
                    rec.op("vector", lambda v, SM=SM, SR=SR, S4=S4: v.tensor_mul(
                        stats[:, S4:S4 + 1], stats[:, SM:SM + 1], stats[:, SR:SR + 1]), inc="V")
                    rec.op("vector", lambda v, S4=S4, SN=SN: v.tensor_scalar_mul(
                        stats[:, SN:SN + 1], stats[:, S4:S4 + 1], -1.0), inc="V")
                    if KDUMP and i < 2:
                        rec.op("vector", lambda v, i=i: v.tensor_copy(
                            stash[:, i * 32:(i + 1) * 32], stats[:, 0:32]))
                    ca = rec.op("vector", lambda v, SR=SR, SN=SN: v.tensor_scalar(
                        xn_tm[:, :], x_tile[:, :], stats[:, SR:SR + 1], stats[:, SN:SN + 1],
                        ALU.mult, ALU.add), inc="V")
                    rec.wait("tensor", "V", ca)
                    for fc in range(NF):
                        rec.op("tensor", lambda t, fc=fc: t.transpose(
                            ln_ps[:, fc * 128:(fc + 1) * 128],
                            xn_tm[:, fc * 128:(fc + 1) * 128], ident_bf[:, :]))
                    ct = rec.bump("tensor")
                    last_ct[0] = ct
                    rec.wait("vector", "T", ct)
                    cvv = rec.op("vector", lambda v, i=i: v.tensor_copy(
                        XN3[:, :, i * 128:(i + 1) * 128],
                        ln_ps[:, :].rearrange("p (c t) -> p c t", c=16)), inc="V")
                    rec.wait("tensor", "V", cvv)
                    rec.wait("sync", "V", cvv)
                    if KSERIAL:
                        rec.barrier()
            rec.barrier()

        def block_L0_attn():
            Wqkv, bqkv = inp["L0_attn_Wqkv"], inp["L0_attn_bqkv"]
            Wout, bout = inp["L0_attn_Wout"], inp["L0_attn_bout"]
            slopes = inp["L0_attn_slopes"]

            with ExitStack() as bctx:
                sbl = lambda name, shape, dt: bctx.enter_context(nc.sbuf_tensor(name, shape, dt))
                difft = sbl("difft", [64, 64], F32)
                maskt64 = sbl("maskt64", [64, 64], F32)
                iota64r = sbl("iota64r", [1, 64], F32)
                rev64c = sbl("rev64c", [64, 1], F32)
                slopes_sb = sbl("slopes_sb", [16, 1], F32)
                s_row = sbl("s_row", [1, 16], F32)
                negs64 = sbl("negs64", [64, 16], F32)
                decT = sbl("decT", [64, 16 * 64], BF16)
                qdec = sbl("qdec", [128, 16 * 64], BF16)
                kdec = sbl("kdec", [64, 16], F32)
                rev512 = sbl("rev512", [1, 512], BF16)
                krow512 = sbl("krow512", [1, 512], F32)
                kdec_bc = sbl("kdec_bc", [128, 512], F32)
                kdk_fm = sbl("kdk_fm", [128, 2048], BF16)
                bd_col = sbl("bd_col", [128, 16], F32)
                bqkv_cols = sbl("bqkv_cols", [128, 48], F32)
                vbias_bc = sbl("vbias_bc", [128, 128], F32)
                q_fm = sbl("q_fm", [128, 2048], BF16)
                k_fm = sbl("k_fm", [128, 2048], BF16)
                v_st = sbl("v_st", [64, 32 * 128], BF16)
                slab = sbl("slab", [128, 16 * 128], BF16)
                scT_bf = sbl("scT_bf", [64, 2 * 64], BF16)
                kdk = sbl("kdk", [64, 2 * 128], BF16)
                qdq = sbl("qdq", [128, 2 * 64], F32)
                S_f32 = sbl("S_f32", [128, 128], F32)
                qrow = sbl("qrow", [1, 64], F32)

                load_const("DIFFT64", difft)
                load_const("MASKT64", maskt64)
                load_const("IOTA64_ROW", iota64r)
                load_const("REV64_COL", rev64c)
                load_const("REV512_ROW", rev512)
                rec.dma("sync", slopes_sb[:, :], slopes.rearrange("(h o) -> h o", o=1))
                rec.dma("sync", bqkv_cols[:, :], bqkv.rearrange("(c p) -> p c", p=128),
                        allow_slow_non_contiguous=True)
                c_bq = rec.dma("sync", brow[:, :], bqkv.rearrange("(o c) -> o c", o=1)[:, 4096:6144])
                rec.barrier()

                if KPH >= 1:
                  with nc.psum_tensor("dk_ps", [128, 64], F32) as dk_ps, \
                       nc.psum_tensor("dk_ps2", [64, 1], F32) as dk_ps2, \
                       nc.psum_tensor("sr_ps", [1, 16], F32) as sr_ps:
                      ct = rec.op("tensor", lambda t: t.transpose(
                          sr_ps[:, :], slopes_sb[:, :], ident_f32[0:16, 0:16]), inc="T")
                      rec.wait("vector", "T", ct)
                      cv = rec.op("vector", lambda v: v.tensor_copy(s_row[:, :], sr_ps[:, :]), inc="V")
                      rec.wait("tensor", "V", cv)
                      for h in range(H0):
                          ct = rec.op("tensor", lambda t, h=h: t.matmul(
                              dk_ps2[:, :], ones_1x64_f32[:, :], s_row[:, h:h + 1],
                              start=True, stop=True), inc="T")
                          rec.wait("vector", "T", ct)
                          cv = rec.op("vector", lambda v, h=h: v.tensor_scalar(
                              negs64[:, h:h + 1], dk_ps2[:, :], -1.0, None, ALU.mult), inc="V")
                          rec.wait("scalar", "V", cv)
                          rec.wait("tensor", "V", cv)
                          ca = rec.op("scalar", lambda s, h=h: s.activation(
                              decT[:, h * 64:(h + 1) * 64], difft[:, :], AF.Exp,
                              scale=negs64[0:64, h:h + 1]), inc="A")
                          rec.wait("vector", "A", ca)
                          rec.op("vector", lambda v, h=h: v.tensor_tensor(
                              decT[:, h * 64:(h + 1) * 64], decT[:, h * 64:(h + 1) * 64],
                              maskt64[:, :], ALU.mult))
                          rec.op("scalar", lambda s, h=h: s.activation(
                              kdec[:, h:h + 1], rev64c[:, :], AF.Exp,
                              scale=negs64[0:64, h:h + 1]))
                          ca = rec.op("scalar", lambda s, h=h: s.activation(
                              qrow[:, :], iota64r[:, :], AF.Exp,
                              scale=negs64[0:1, h:h + 1]), inc="A")
                          rec.wait("tensor", "A", ca)
                          ct = rec.op("tensor", lambda t, h=h: t.matmul(
                              dk_ps[:, :], ones_1x128_f32[:, :], qrow[:, :],
                              start=True, stop=True), inc="T")
                          rec.wait("vector", "T", ct)
                          cv = rec.op("vector", lambda v, h=h: v.tensor_copy(
                              qdec[:, h * 64:(h + 1) * 64], dk_ps[:, :]), inc="V")
                          rec.wait("tensor", "V", cv)
                          ct = rec.op("tensor", lambda t, h=h: t.matmul(
                              dk_ps[:, 0:1], ones_1x128_f32[:, :], kdec[0:1, h:h + 1],
                              start=True, stop=True), inc="T")
                          rec.wait("vector", "T", ct)
                          cv = rec.op("vector", lambda v, h=h: v.tensor_copy(
                              bd_col[:, h:h + 1], dk_ps[:, 0:1]), inc="V")
                          rec.wait("tensor", "V", cv)
                rec.barrier()

                if KPH >= 2:
                    ln_pass(inp["x_in"])

                attn_fm = A2
                slab3 = slab.ap().rearrange("p (c o) -> p c o", o=128)
                if KPH >= 3:
                  with nc.psum_tensor("ps_arena", [128, 4096], F32) as PSA_h:
                    PSA = PSA_h[:, :]
                    PSB = PSA_h.bitcast(BF16)[:, :]
                    pq_banks = (PSA[:, 0 * 512:1 * 512], PSA[:, 1 * 512:2 * 512])
                    pv_banks = (PSA[:, 2 * 512:2 * 512 + 128], PSA[:, 3 * 512:3 * 512 + 128])
                    sc_banks = (PSA[0:64, 0 * 512:0 * 512 + 64], PSA[0:64, 1 * 512:1 * 512 + 64])
                    ktm_banks = (PSB[0:64, 2 * 1024:2 * 1024 + 128], PSB[0:64, 3 * 1024:3 * 1024 + 128])
                    dl_banks = (PSA[:, 4 * 512:4 * 512 + 128], PSA[:, 5 * 512:5 * 512 + 128])
                    o_banks = (PSA[:, 6 * 512:6 * 512 + 64], PSA[:, 7 * 512:7 * 512 + 64])

                    for h in range(min(H0, KNH)):
                        for which in ("q", "k"):
                            col0 = h * 128 if which == "q" else 2048 + h * 128
                            dst = q_fm if which == "q" else k_fm
                            bcol = h if which == "q" else 16 + h
                            c_sl = rec.dma("gpsimd", slab3,
                                           Wqkv[:, col0:col0 + 128].rearrange("(c p) o -> p c o", p=128))
                            if which == "k":
                                cak = rec.op("scalar", lambda s, h=h: s.activation(
                                    krow512[:, :], rev512[:, :], AF.Exp,
                                    scale=negs64[0:1, h:h + 1]), inc="A")
                                rec.wait("tensor", "A", cak)
                                ctk = rec.op("tensor", lambda t: t.matmul(
                                    pq_banks[0][:, :], ones_1x128_f32[:, :], krow512[:, :],
                                    start=True, stop=True), inc="T")
                                rec.wait("vector", "T", ctk)
                                cvk = rec.op("vector", lambda v: v.tensor_copy(
                                    kdec_bc[:, :], pq_banks[0][:, :]), inc="V")
                                rec.wait("tensor", "V", cvk)
                            rec.wait("tensor", "GD", c_sl)
                            for g in range(4):
                                ps = pq_banks[g % 2]
                                for fc in range(NF):
                                    rec.op("tensor", lambda t, fc=fc, g=g, ps=ps: t.matmul(
                                        ps[:, :], slab3[:, fc, :], XN3[:, fc, g * 512:(g + 1) * 512],
                                        start=(fc == 0), stop=(fc == NF - 1)))
                                ct = rec.bump("tensor")
                                rec.wait("vector", "T", ct)
                                ca = rec.op("vector", lambda v, g=g, ps=ps, dst=dst, bcol=bcol:
                                            v.tensor_scalar(dst[:, g * 512:(g + 1) * 512], ps[:, :],
                                                            bqkv_cols[:, bcol:bcol + 1], None,
                                                            ALU.add), inc="V")
                                if which == "k":
                                    ca = rec.op("vector", lambda v, g=g, ps=ps, bcol=bcol:
                                                v.scalar_tensor_tensor(
                                                    kdk_fm[:, g * 512:(g + 1) * 512], ps[:, :],
                                                    bqkv_cols[:, bcol:bcol + 1], kdec_bc[:, :],
                                                    ALU.add, ALU.mult), inc="V")
                                if g >= 1:
                                    rec.wait("tensor", "V", ca - 1)
                            rec.wait("tensor", "V", rec.cnt["V"])
                            rec.wait("gpsimd", "V", rec.cnt["V"])
                        if not KSV:
                            continue
                        c_sl = rec.dma("gpsimd", slab3,
                                       Wqkv[:, 4096 + h * 128:4096 + (h + 1) * 128]
                                       .rearrange("(c p) o -> p c o", p=128))
                        rec.wait("tensor", "SD", c_bq)
                        ct = rec.op("tensor", lambda t, h=h: t.matmul(
                            pv_banks[1][:, :], ones_1x128_f32[:, :],
                            brow[:, h * 128:(h + 1) * 128],
                            start=True, stop=True), inc="T")
                        rec.wait("vector", "T", ct)
                        cv = rec.op("vector", lambda v: v.tensor_copy(vbias_bc[:, :], pv_banks[1][:, :]),
                                    inc="V")
                        rec.wait("tensor", "V", cv)
                        rec.wait("tensor", "GD", c_sl)
                        for i in range(NT):
                            ps = pv_banks[i % 2]
                            for fc in range(NF):
                                rec.op("tensor", lambda t, fc=fc, i=i, ps=ps: t.matmul(
                                    ps[:, :], XN3[:, fc, i * 128:(i + 1) * 128], slab3[:, fc, :],
                                    start=(fc == 0), stop=(fc == NF - 1)))
                            ct = rec.bump("tensor")
                            rec.wait("vector", "T", ct)
                            rec.op("vector", lambda v, i=i, ps=ps: v.tensor_tensor(
                                v_st[0:64, (2 * i) * 128:(2 * i + 1) * 128],
                                ps[0:64, :], vbias_bc[0:64, :], ALU.add))
                            cv = rec.op("vector", lambda v, i=i, ps=ps: v.tensor_tensor(
                                v_st[0:64, (2 * i + 1) * 128:(2 * i + 2) * 128],
                                ps[64:128, :], vbias_bc[64:128, :], ALU.add), inc="V")
                            if i >= 1:
                                rec.wait("tensor", "V", cv - 1)
                        cv_last = rec.cnt["V"]
                        rec.wait("tensor", "V", cv_last)
                        rec.wait("gpsimd", "V", cv_last)

                        if KPH < 4:
                            continue
                        dT = decT[:, h * 64:(h + 1) * 64]
                        qd = qdec[:, h * 64:(h + 1) * 64]
                        kd = kdec[:, h:h + 1]
                        bd = bd_col[:, h:h + 1]
                        cv0 = rec.op("vector", lambda v: v.memset(S_f32[:, :], 0.0), inc="V")
                        rec.wait("tensor", "V", cv0)
                        stt = {}
                        oev = {}
                        dvk = {}
                        tdl = {}
                        tsc = {}
                        to4 = {}
                        for c in range(NCH):
                            p = c % 2
                            if c >= 2:
                                rec.wait("tensor", "V", dvk[c - 2])
                            rec.op("tensor", lambda t, c=c, p=p: t.matmul(
                                sc_banks[p], k_fm[:, c * 64:(c + 1) * 64],
                                q_fm[:, c * 64:(c + 1) * 64], start=True, stop=True))
                            rec.op("tensor", lambda t, c=c, p=p: t.transpose(
                                ktm_banks[p], kdk_fm[:, c * 64:(c + 1) * 64],
                                ident_bf[:, :]))
                            tsc[c] = rec.bump("tensor")
                            rec.wait("vector", "T", tsc[c])
                            rec.op("vector", lambda v, c=c, p=p, dT=dT: v.tensor_tensor(
                                scT_bf[:, p * 64:(p + 1) * 64], sc_banks[p],
                                dT, ALU.mult))
                            rec.op("vector", lambda v, c=c, p=p: v.tensor_copy(
                                kdk[:, p * 128:(p + 1) * 128], ktm_banks[p]))
                            dvk[c] = rec.op("vector", lambda v, c=c, p=p, qd=qd: v.tensor_tensor(
                                qdq[:, p * 64:(p + 1) * 64], q_fm[:, c * 64:(c + 1) * 64],
                                qd, ALU.mult), inc="V")
                            rec.wait("tensor", "V", dvk[c])
                            if c >= 2:
                                rec.wait("tensor", "V", stt[c - 2])
                            rec.op("tensor", lambda t, c=c, p=p: t.matmul(
                                dl_banks[p], kdk[:, p * 128:(p + 1) * 128],
                                v_st[0:64, c * 128:(c + 1) * 128], start=True, stop=True))
                            tdl[c] = rec.bump("tensor")
                            if c >= 2:
                                rec.wait("tensor", "A", oev[c - 2])
                            if c >= 1:
                                rec.wait("tensor", "V", stt[c - 1])
                            rec.op("tensor", lambda t, c=c, p=p: t.matmul(
                                o_banks[p], v_st[0:64, c * 128:(c + 1) * 128],
                                scT_bf[0:64, p * 64:(p + 1) * 64], start=True, stop=(c == 0)))
                            if c >= 1:
                                rec.op("tensor", lambda t, c=c, p=p: t.matmul(
                                    o_banks[p], S_f32[:, :],
                                    qdq[:, p * 64:(p + 1) * 64], start=False, stop=True))
                            to4[c] = rec.bump("tensor")
                            rec.wait("vector", "T", to4[c])
                            stt[c] = rec.op("vector", lambda v, c=c, p=p, bd=bd: v.scalar_tensor_tensor(
                                S_f32[:, :], S_f32[:, :], bd, dl_banks[p],
                                ALU.mult, ALU.add), inc="V")
                            rec.wait("scalar", "T", to4[c])
                            oev[c] = rec.op("scalar", lambda s, c=c, p=p, h=h: s.copy(
                                attn_fm[:, h * 2048 + c * 64: h * 2048 + (c + 1) * 64],
                                o_banks[p]), inc="A")
                        rec.wait("tensor", "V", rec.cnt["V"])
                        rec.wait("tensor", "A", rec.cnt["A"])
                        if KDUMP == 5 and h == 0:
                            cvd = rec.op("vector", lambda v: v.tensor_copy(
                                x_new[:, :], attn_fm.ap()[:, 0:2048]), inc="V")
                            rec.wait("sync", "V", cvd)
                            cds = rec.dma("sync", out_ap[1792:1920, :], x_new[:, :])
                            rec.wait("vector", "SD", cds)
                            rec.barrier()
                rec.barrier()

                if KDUMP == 2:
                    X16d = XN.ap().rearrange("p (a u) -> p a u", a=16)
                    for c in range(8):
                        cvd = rec.op("vector", lambda v, c=c: v.tensor_copy(
                            x_new[:, :], X16d[:, c, :]), inc="V")
                        rec.wait("sync", "V", cvd)
                        cds = rec.dma("sync", out_ap[c * 128:(c + 1) * 128, :], x_new[:, :])
                        rec.wait("vector", "SD", cds)
                    rec.barrier()
                if KDUMP == 5:
                    for hh in range(2):
                        cvd = rec.op("vector", lambda v, hh=hh: v.tensor_copy(
                            x_new[:, :], attn_fm.ap()[:, hh * 2048:(hh + 1) * 2048]), inc="V")
                        rec.wait("sync", "V", cvd)
                        cds = rec.dma("sync", out_ap[hh * 128:(hh + 1) * 128, :], x_new[:, :])
                        rec.wait("vector", "SD", cds)
                    rec.barrier()
                if KDUMP == 4:
                    def sd4(row0, ap_, rows=64):
                        cvd = rec.op("vector", lambda v, a=ap_, r=rows: v.tensor_copy(x_new[0:r, 0:a.shape[-1]], a), inc="V")
                        rec.wait("sync", "V", cvd)
                        cds = rec.dma("sync", out_ap[row0:row0 + rows, 0:ap_.shape[-1]], x_new[0:rows, 0:ap_.shape[-1]])
                        rec.wait("vector", "SD", cds)
                    sd4(0, decT[:, :], 64)
                    sd4(64, qdec.ap()[:, :], 128)
                    sd4(192, kdec[:, :], 64)
                    sd4(256, bd_col[:, :], 128)
                    sd4(384, negs64[:, :], 64)
                    sd4(448, s_row[:, :], 1)
                    rec.barrier()
                if KDUMP == 3:
                    for hh in range(16):
                        cvd = rec.op("vector", lambda v, hh=hh: v.tensor_copy(
                            x_new[:, :], attn_fm.ap()[:, hh * 2048:(hh + 1) * 2048]), inc="V")
                        rec.wait("sync", "V", cvd)
                        cds = rec.dma("sync", out_ap[hh * 128:(hh + 1) * 128, :], x_new[:, :])
                        rec.wait("vector", "SD", cds)
                    rec.barrier()
                if KDUMP == 1:
                    def safe_dump(row0, src_ap):
                        cvd = rec.op("vector", lambda v, s=src_ap: v.tensor_copy(x_new[:, :], s), inc="V")
                        rec.wait("sync", "V", cvd)
                        cds = rec.dma("sync", out_ap[row0:row0 + 128, :], x_new[:, :])
                        rec.wait("vector", "SD", cds)
                    safe_dump(0 * 128, q_fm[:, :])
                    safe_dump(1 * 128, k_fm[:, :])
                    cvd = rec.op("vector", lambda v: v.tensor_copy(x_new[0:64, :], v_st[0:64, 0:2048]), inc="V")
                    rec.wait("sync", "V", cvd)
                    cds = rec.dma("sync", out_ap[2 * 128:2 * 128 + 64, :], x_new[0:64, :])
                    rec.wait("vector", "SD", cds)
                    cvd = rec.op("vector", lambda v: v.tensor_copy(x_new[0:64, :], v_st[0:64, 2048:4096]), inc="V")
                    rec.wait("sync", "V", cvd)
                    cds = rec.dma("sync", out_ap[2 * 128 + 64:3 * 128, :], x_new[0:64, :])
                    rec.wait("vector", "SD", cds)
                    safe_dump(3 * 128, attn_fm.ap()[:, 15 * 2048:16 * 2048])
                    rec.barrier()

                if KPH < 5:
                    return
                W_res = A1
                W3 = W_res.ap().rearrange("p (c o) -> p c o", c=16)
                c_w = rec.dma("gpsimd", W3, Wout.rearrange("(c p) o -> p c o", p=128))
                rec.wait("tensor", "GD", c_w)
                c_br = rec.dma("sync", brow[0:1, 0:2048], bout.rearrange("(o c) -> o c", o=1))
                with nc.psum_tensor("bb_ps", [128, 2048], F32) as bb_ps:
                    rec.wait("tensor", "SD", c_br)
                    for j in range(4):
                        rec.op("tensor", lambda t, j=j: t.matmul(
                            bb_ps[:, j * 512:(j + 1) * 512], ones_1x128_f32[:, :],
                            brow[0:1, j * 512:(j + 1) * 512], start=True, stop=True))
                    ct = rec.bump("tensor")
                    rec.wait("vector", "T", ct)
                    cv = rec.op("vector", lambda v: v.tensor_copy(bc_f32[:, :], bb_ps[:, :]), inc="V")
                rec.barrier()

                A23 = attn_fm.ap().rearrange("p (hh t) -> p hh t", hh=16)
                with nc.psum_tensor("op_ps", [128, 512], F32) as op_ps, \
                     nc.psum_tensor("op_ps2", [128, 512], F32) as op_ps2:
                    evs = []
                    for i in range(NT):
                        c_ld = rec.dma("sync", x_tile[:, :], inp["x_in"][i * 128:(i + 1) * 128, :])
                        rec.wait("vector", "SD", c_ld)
                        cxb = rec.op("vector", lambda v: v.tensor_tensor(
                            x_new[:, :], x_tile[:, :], bc_f32[:, :], ALU.add), inc="V")
                        for fo in range(4):
                            ps = op_ps if fo % 2 == 0 else op_ps2
                            if len(evs) >= 2:
                                rec.wait("tensor", "V", evs[-2])
                            for fi in range(NF):
                                rec.op("tensor", lambda t, fi=fi, fo=fo, i=i, ps=ps: t.matmul(
                                    ps[:, :], A23[:, fi, i * 128:(i + 1) * 128],
                                    W3[:, fi, fo * 512:(fo + 1) * 512],
                                    start=(fi == 0), stop=(fi == NF - 1)))
                            ct = rec.bump("tensor")
                            rec.wait("vector", "T", ct)
                            cv = rec.op("vector", lambda v, fo=fo, ps=ps: v.tensor_tensor(
                                x_new[:, fo * 512:(fo + 1) * 512], x_new[:, fo * 512:(fo + 1) * 512],
                                ps[:, :], ALU.add), inc="V")
                            evs.append(cv)
                        rec.wait("gpsimd", "V", evs[-1])
                        c_st = rec.dma("gpsimd", X[i * 128:(i + 1) * 128, :], x_new[:, :])
                        rec.wait("vector", "GD", c_st)
                        rec.wait("sync", "V", cxb)
                rec.barrier()


        def block_L0_ffn():
            rw, rb = inp["L0_ffn_router_w"], inp["L0_ffn_router_b"]
            ew, eb = inp["L0_ffn_expert_w"], inp["L0_ffn_expert_b"]

            with ExitStack() as bctx:
                sbl = lambda name, shape, dt: bctx.enter_context(nc.sbuf_tensor(name, shape, dt))
                rw_sb = sbl("rw_sb", [128, 16 * 16], BF16)
                rb_bc = sbl("rb_bc", [128, 16], F32)
                S_local = sbl("S_local", [128, 16 * 16], F32)
                S_rowT = sbl("S_rowT", [16, 2048], F32)
                S_big = sbl("S_big", [128, 2048], F32)
                grp_sb = sbl("grp_sb", [128, 16], F32)
                grp2_sb = sbl("grp2_sb", [16, 128], F32)
                bis = sbl("bis", [128, 8], F32)
                cnt16_sb = sbl("cnt16_sb", [16, 1], F32)
                updm = sbl("updm", [128, 1], mybir.dt.uint8)
                thr_row = sbl("thr_row", [1, 16], F32)
                thr_bc = sbl("thr_bc", [128, 16], F32)
                gsum_all = sbl("gsum_all", [128, 16], F32)
                gscr = sbl("gscr", [128, 16], F32)
                t1_scr = sbl("t1_scr", [128, 512], F32)

                c_rw = rec.dma("gpsimd", rw_sb[:, :].rearrange("p (c e) -> p c e", e=16),
                               rw.rearrange("(c p) e -> p c e", p=128))
                c_rb = rec.dma("sync", brow[0:1, 0:16], rb.rearrange("(o c) -> o c", o=1))
                load_const("GRP", grp_sb)
                load_const("GRP2", grp2_sb)
                rec.barrier()

                ln_pass(X)
                rw3 = rw_sb.ap().rearrange("p (c e) -> p c e", e=16)

                with nc.psum_tensor("r_psA", [128, 4096], F32) as r_psA:
                    r_ps = (r_psA[:, 0:16], r_psA[:, 512:528])
                    bb_ps = r_psA[:, 1024:1040]
                    rec.wait("tensor", "SD", c_rb)
                    ct = rec.op("tensor", lambda t: t.matmul(
                        bb_ps, ones_1x128_f32[:, :], brow[0:1, 0:16],
                        start=True, stop=True), inc="T")
                    rec.wait("vector", "T", ct)
                    rec.op("vector", lambda v: v.tensor_copy(rb_bc[:, :], bb_ps), inc="V")
                    evr = {}
                    for i in range(NT):
                        ps = r_ps[i % 2]
                        if i >= 2:
                            rec.wait("tensor", "V", evr[i - 2])
                        for fc in range(NF):
                            rec.op("tensor", lambda t, fc=fc, i=i, ps=ps: t.matmul(
                                ps, XN3[:, fc, i * 128:(i + 1) * 128], rw3[:, fc, :],
                                start=(fc == 0), stop=(fc == NF - 1)))
                        ct = rec.bump("tensor")
                        rec.wait("vector", "T", ct)
                        rec.op("vector", lambda v, i=i, ps=ps: v.tensor_tensor(
                            gscr[:, :], ps, rb_bc[:, :], ALU.add), inc="V")
                        rec.op("vector", lambda v: v.reduce_max(
                            bis[:, 5:6], gscr[:, :], axis=AX.X), inc="V")
                        cvm = rec.op("vector", lambda v: v.tensor_scalar_mul(
                            bis[:, 6:7], bis[:, 5:6], -1.0), inc="V")
                        rec.wait("scalar", "V", cvm)
                        cae = rec.op("scalar", lambda s: s.activation(
                            S_local[0:128, 0:16] if False else gscr[:, :], gscr[:, :], AF.Exp,
                            bias=bis[:, 6:7], accum_out=bis[:, 7:8]), inc="A")
                        rec.wait("vector", "A", cae)
                        rec.op("vector", lambda v: v.reciprocal(bis[:, 4:5], bis[:, 7:8]), inc="V")
                        evr[i] = rec.op("vector", lambda v, i=i: v.tensor_scalar(
                            S_local[:, i * 16:(i + 1) * 16], gscr[:, :], bis[:, 4:5], None,
                            ALU.mult), inc="V")
                    tr_ps = (r_psA[:, 1536:1664], r_psA[:, 2048:2176])
                    evt = {}
                    for i in range(NT):
                        ps = tr_ps[i % 2]
                        if i >= 2:
                            rec.wait("tensor", "V", evt[i - 2])
                        rec.op("tensor", lambda t, i=i, ps=ps: t.transpose(
                            ps[0:16, :], S_local[:, i * 16:(i + 1) * 16], ident_f32[:, :]))
                        ct = rec.bump("tensor")
                        rec.wait("vector", "T", ct)
                        evt[i] = rec.op("vector", lambda v, i=i, ps=ps: v.tensor_copy(
                            S_rowT[:, i * 128:(i + 1) * 128], ps[0:16, :]), inc="V")
                    cvl = rec.cnt["V"]
                    rec.wait("gpsimd", "V", cvl)
                    c_ci = rec.dma("gpsimd", CC_IN.ap(), S_rowT[:, :])
                    rec.wait("gpsimd", "GD", c_ci)
                    rec.op("gpsimd", lambda g: g.collective_compute(
                        "AllGather", ALU.bypass,
                        replica_groups=[list(range(N_CORES))],
                        ins=[CC_IN.ap().opt()],
                        outs=[CC_OUT.ap().opt()]), inc="CC", inc_n=1)
                    rec.wait("gpsimd", "CC", 1)
                    c_sb = rec.dma("gpsimd", S_big[:, :], CC_OUT.ap())
                    rec.wait("vector", "GD", c_sb)
                    rec.wait("tensor", "GD", c_sb)

                    b_ps16 = r_psA[:, 2560:2561]
                    b_ps128 = r_psA[:, 3072:3073]
                    rec.op("vector", lambda v: v.memset(bis[:, 0:1], 0.0), inc="V")
                    cvh = rec.op("vector", lambda v: v.memset(bis[:, 1:2], 1.0), inc="V")
                    for it in range(BISECT_ITERS):
                        rec.op("vector", lambda v: v.tensor_scalar(
                            bis[:, 2:3], bis[:, 0:1], bis[:, 1:2], 0.5,
                            ALU.add, ALU.mult), inc="V")
                        cvc = rec.op("vector", lambda v: v.tensor_scalar(
                            xsq_scr[:, :], S_big[:, :], bis[:, 2:3], 0.0,
                            ALU.is_ge, ALU.add, accum_out=bis[:, 3:4]), inc="V")
                        rec.wait("tensor", "V", cvc)
                        ct1 = rec.op("tensor", lambda t: t.matmul(
                            b_ps16[0:16, :], grp_sb[:, :], bis[:, 3:4],
                            start=True, stop=True), inc="T")
                        rec.wait("vector", "T", ct1)
                        cv16 = rec.op("vector", lambda v: v.tensor_copy(
                            cnt16_sb[:, :], b_ps16[0:16, :]), inc="V")
                        rec.wait("tensor", "V", cv16)
                        ct2 = rec.op("tensor", lambda t: t.matmul(
                            b_ps128, grp2_sb[:, :], cnt16_sb[:, :],
                            start=True, stop=True), inc="T")
                        rec.wait("vector", "T", ct2)
                        rec.op("vector", lambda v: v.tensor_scalar(
                            updm[:, :], b_ps128, float(TOPK), None, ALU.is_ge), inc="V")
                        rec.op("vector", lambda v: v.select(
                            bis[:, 0:1], updm[:, :], bis[:, 2:3], bis[:, 0:1]), inc="V")
                        rec.op("vector", lambda v: v.select(
                            bis[:, 1:2], updm[:, :], bis[:, 1:2], bis[:, 2:3]), inc="V")
                    cvt = rec.cnt["V"]
                    rec.wait("tensor", "V", cvt)
                    ct = rec.op("tensor", lambda t: t.transpose(
                        b_ps16[0:1, 0:16] if False else r_psA[0:1, 2560:2576],
                        bis[0:16, 0:1], ident_f32[0:16, 0:16]), inc="T")
                    rec.wait("vector", "T", ct)
                    cvr = rec.op("vector", lambda v: v.tensor_copy(
                        thr_row[:, :], r_psA[0:1, 2560:2576]), inc="V")
                    rec.wait("tensor", "V", cvr)
                    ct = rec.op("tensor", lambda t: t.matmul(
                        r_psA[:, 3584:3600], ones_1x128_f32[:, :], thr_row[:, :],
                        start=True, stop=True), inc="T")
                    rec.wait("vector", "T", ct)
                    rec.op("vector", lambda v: v.tensor_copy(thr_bc[:, :], r_psA[:, 3584:3600]), inc="V")
                    for i in range(NT):
                        rec.op("vector", lambda v, i=i: v.tensor_tensor(
                            gscr[:, :], S_local[:, i * 16:(i + 1) * 16], thr_bc[:, :], ALU.is_ge), inc="V")
                        rec.op("vector", lambda v, i=i: v.tensor_tensor(
                            gscr[:, :], gscr[:, :], S_local[:, i * 16:(i + 1) * 16], ALU.mult), inc="V")
                        rec.op("vector", lambda v, i=i: v.reduce_sum(
                            gsum_all[:, i:i + 1], gscr[:, :], axis=AX.X), inc="V")
                rec.barrier()

                W_res = A2
                W3e = W_res.ap().rearrange("p (c o) -> p c o", c=16)
                c_w = rec.dma("gpsimd", W3e, ew.rearrange("(c p) o -> p c o", p=128))
                rec.wait("tensor", "GD", c_w)
                c_br = rec.dma("sync", brow[0:1, 0:2048], eb.rearrange("(o c) -> o c", o=1))
                with nc.psum_tensor("eb_ps", [128, 2048], F32) as eb_ps:
                    rec.wait("tensor", "SD", c_br)
                    for j in range(4):
                        rec.op("tensor", lambda t, j=j: t.matmul(
                            eb_ps[:, j * 512:(j + 1) * 512], ones_1x128_f32[:, :],
                            brow[0:1, j * 512:(j + 1) * 512], start=True, stop=True))
                    ct = rec.bump("tensor")
                    rec.wait("vector", "T", ct)
                    rec.op("vector", lambda v: v.tensor_copy(bc_f32[:, :], eb_ps[:, :]), inc="V")
                rec.barrier()

                with nc.psum_tensor("e_ps", [128, 1024], F32) as e_ps:
                    ep = (e_ps[:, 0:512], e_ps[:, 512:1024])
                    evs = []
                    for i in range(NT):
                        c_ld = rec.dma("sync", x_tile[:, :], X[i * 128:(i + 1) * 128, :])
                        rec.wait("vector", "SD", c_ld)
                        for fo in range(4):
                            ps = ep[fo % 2]
                            if len(evs) >= 2:
                                rec.wait("tensor", "V", evs[-2])
                            for fc in range(NF):
                                rec.op("tensor", lambda t, fc=fc, fo=fo, i=i, ps=ps: t.matmul(
                                    ps, XN3[:, fc, i * 128:(i + 1) * 128],
                                    W3e[:, fc, fo * 512:(fo + 1) * 512],
                                    start=(fc == 0), stop=(fc == NF - 1)))
                            ct = rec.bump("tensor")
                            rec.wait("vector", "T", ct)
                            rec.op("vector", lambda v, fo=fo, ps=ps: v.tensor_tensor(
                                t1_scr[:, :], ps, bc_f32[:, fo * 512:(fo + 1) * 512], ALU.add))
                            cv = rec.op("vector", lambda v, fo=fo, i=i: v.scalar_tensor_tensor(
                                x_new[:, fo * 512:(fo + 1) * 512], t1_scr[:, :],
                                gsum_all[:, i:i + 1], x_tile[:, fo * 512:(fo + 1) * 512],
                                ALU.mult, ALU.add), inc="V")
                            evs.append(cv)
                        rec.wait("gpsimd", "V", evs[-1])
                        c_st = rec.dma("gpsimd", X[i * 128:(i + 1) * 128, :], x_new[:, :])
                        rec.wait("vector", "GD", c_st)
                        rec.wait("sync", "V", evs[-1])
                rec.barrier()

        if upto >= 1:
            block_L0_attn()

        def block_L1_attn():
            H1, KVH = 16, 4
            SCALE = float(HD) ** -0.5
            with ExitStack() as bctx:
                sbl = lambda name, shape, dt: bctx.enter_context(nc.sbuf_tensor(name, shape, dt))
                slab = sbl("l1slab", [128, 16 * 512], BF16)
                cos_s = sbl("cos_s", [128, 512], F32)
                sin_s = sbl("sin_s", [128, 512], F32)
                rot_sb = sbl("rot_sb", [128, 128], F32)
                masks_sb = sbl("masks_sb", [128, 4 * 512], BF16)
                qpre = sbl("qpre", [128, 512], F32)
                t2s = sbl("t2s", [128, 512], F32)
                t3s = sbl("t3s", [128, 512], F32)
                p_sb = sbl("p_sb", [128, 2 * 512], BF16)
                rb_sb = sbl("rb_sb", [128, 512], F32)
                r_row = sbl("r_row", [1, 512], F32)
                kn2m = sbl("kn2m", [1, 8], F32)
                csc = sbl("csc", [1, 8], F32)
                negc_col = sbl("negc_col", [128, 1], F32)

                load_const("ROT_F32", rot_sb)
                for di, delta in enumerate((128, 256, 384, 512)):
                    load_const(f"MASKT_{delta}", masks_sb[:, di * 512:(di + 1) * 512]) if False else None
                    rec.dma("sync", masks_sb[:, di * 512:(di + 1) * 512], inp[f"MASKT_{delta}"])
                rec.barrier()

                ln_pass(X)

                Karea = A2.ap()[:, 0:8192]
                Varea = A2.ap()[:, 8192:16384]
                Qarea = A2.ap()[:, 16384:24576]
                Tarea = A2.ap()[:, 24576:32768]
                slab3 = slab.ap().rearrange("p (c o) -> p c o", o=512)

                with nc.psum_tensor("l1_ps", [128, 4096], F32) as PS:
                    bS = (PS[:, 0:512], PS[:, 512:1024])
                    bO = PS[:, 1024:1536]
                    bL = PS[0:1, 1536:2048]
                    bQ = PS[0:1, 1536 + 0:1536 + 512]
                    bR = PS[:, 2048:2560]
                    bN = PS[:, 2560:2561]
                    bY = (PS[:, 3072:3584], PS[:, 3584:4096])

                    for fc in range(NF):
                        rec.dma("sync", slab3[:, fc, :], WK_BF[0:512, fc * 128:(fc + 1) * 128],
                                transpose=True)
                    c_wk = rec.cnt["SD"]
                    rec.wait("tensor", "SD", c_wk)
                    for g in range(4):
                        c_cs = rec.dma("sync", cos_s[:, :], inp["COS_FM"][:, g * 512:(g + 1) * 512])
                        c_sn = rec.dma("sync", sin_s[:, :], inp["SIN_FM"][:, g * 512:(g + 1) * 512])
                        rec.wait("vector", "SD", c_sn)
                        for kh in range(KVH):
                            ps = bS[kh % 2]
                            for fc in range(NF):
                                rec.op("tensor", lambda t, fc=fc, g=g, kh=kh, ps=ps: t.matmul(
                                    ps, slab3[:, fc, kh * 128:(kh + 1) * 128],
                                    XN3[:, fc, g * 512:(g + 1) * 512],
                                    start=(fc == 0), stop=(fc == NF - 1)))
                            ct = rec.bump("tensor")
                            rec.wait("vector", "T", ct)
                            cv = rec.op("vector", lambda v, ps=ps: v.tensor_copy(qpre[:, :], ps), inc="V")
                            rec.wait("tensor", "V", cv)
                            ct2 = rec.op("tensor", lambda t: t.matmul(
                                bR, rot_sb[:, :], qpre[:, :], start=True, stop=True), inc="T")
                            rec.wait("vector", "T", ct2)
                            rec.op("vector", lambda v: v.tensor_tensor(
                                t2s[:, :], bR, sin_s[:, :], ALU.mult))
                            rec.op("vector", lambda v: v.tensor_tensor(
                                t3s[:, :], qpre[:, :], cos_s[:, :], ALU.mult))
                            cv2 = rec.op("vector", lambda v, kh=kh, g=g: v.tensor_tensor(
                                Karea[:, kh * 2048 + g * 512: kh * 2048 + (g + 1) * 512],
                                t3s[:, :], t2s[:, :], ALU.add), inc="V")
                            rec.wait("tensor", "V", cv2)
                        rec.wait("sync", "V", rec.cnt["V"])
                    for fc in range(NF):
                        rec.dma("sync", slab3[:, fc, :], WV_BF[0:512, fc * 128:(fc + 1) * 128],
                                transpose=True)
                    c_wv = rec.cnt["SD"]
                    rec.wait("tensor", "SD", c_wv)
                    Vv = Varea.rearrange("p (kh tt d) -> p kh tt d", kh=4, d=128)
                    for tt in range(NT):
                        ps = bS[tt % 2]
                        for fc in range(NF):
                            rec.op("tensor", lambda t, fc=fc, tt=tt, ps=ps: t.matmul(
                                ps, XN3[:, fc, tt * 128:(tt + 1) * 128], slab3[:, fc, :],
                                start=(fc == 0), stop=(fc == NF - 1)))
                        ct = rec.bump("tensor")
                        rec.wait("vector", "T", ct)
                        cv = rec.op("vector", lambda v, tt=tt, ps=ps: v.tensor_copy(
                            Vv[:, :, tt, :], ps.rearrange("p (kh d) -> p kh d", d=128)), inc="V")
                        rec.wait("tensor", "V", cv)
                    cvz = rec.op("vector", lambda v: v.memset(kn2m[:, :], 0.0), inc="V")
                    for kh in range(KVH):
                        for g in range(4):
                            cv = rec.op("vector", lambda v, kh=kh, g=g: v.tensor_tensor(
                                t2s[:, :], Karea[:, kh * 2048 + g * 512: kh * 2048 + (g + 1) * 512],
                                Karea[:, kh * 2048 + g * 512: kh * 2048 + (g + 1) * 512],
                                ALU.mult), inc="V")
                            rec.wait("tensor", "V", cv)
                            ct = rec.op("tensor", lambda t: t.matmul(
                                bQ, ones_128x1_f32[:, :], t2s[:, :], start=True, stop=True), inc="T")
                            rec.wait("vector", "T", ct)
                            rec.op("vector", lambda v: v.reduce_max(
                                csc[:, 0:1], bQ, axis=AX.X), inc="V")
                            rec.op("vector", lambda v, kh=kh: v.tensor_tensor(
                                kn2m[:, kh:kh + 1], kn2m[:, kh:kh + 1], csc[:, 0:1], ALU.max), inc="V")
                    rec.barrier()

                    oev_prev = [0]
                    for ib in range(4):
                        bi = ib * 512
                        c_cs = rec.dma("sync", cos_s[:, :], inp["COS_FM"][:, bi:bi + 512])
                        c_sn = rec.dma("sync", sin_s[:, :], inp["SIN_FM"][:, bi:bi + 512])
                        rec.wait("vector", "SD", c_sn)
                        for hg in range(4):
                            for fc in range(NF):
                                rec.dma("sync", slab3[:, fc, :],
                                        WQ_BF[hg * 512:(hg + 1) * 512, fc * 128:(fc + 1) * 128],
                                        transpose=True)
                            c_wq = rec.cnt["SD"]
                            rec.wait("tensor", "SD", c_wq)
                            for hh in range(4):
                                h = hg * 4 + hh
                                ps = bS[h % 2]
                                for fc in range(NF):
                                    rec.op("tensor", lambda t, fc=fc, hh=hh, ps=ps, bi=bi: t.matmul(
                                        ps, slab3[:, fc, hh * 128:(hh + 1) * 128],
                                        XN3[:, fc, bi:bi + 512],
                                        start=(fc == 0), stop=(fc == NF - 1)))
                                ct = rec.bump("tensor")
                                rec.wait("vector", "T", ct)
                                cv = rec.op("vector", lambda v, ps=ps: v.tensor_copy(qpre[:, :], ps), inc="V")
                                rec.wait("tensor", "V", cv)
                                ct2 = rec.op("tensor", lambda t: t.matmul(
                                    bR, rot_sb[:, :], qpre[:, :], start=True, stop=True), inc="T")
                                rec.wait("vector", "T", ct2)
                                rec.op("vector", lambda v: v.tensor_tensor(
                                    t2s[:, :], bR, sin_s[:, :], ALU.mult))
                                rec.op("vector", lambda v: v.tensor_tensor(
                                    t3s[:, :], qpre[:, :], cos_s[:, :], ALU.mult))
                                cv2 = rec.op("vector", lambda v, h=h: v.tensor_tensor(
                                    Qarea[:, h * 512:(h + 1) * 512],
                                    t3s[:, :], t2s[:, :], ALU.add), inc="V")
                                rec.wait("tensor", "V", cv2)
                            rec.wait("sync", "V", rec.cnt["V"])
                        njc = min((bi + 512 + 128) // 128, 16)
                        for h in range(H1):
                            kh = h // 4
                            rec.wait("vector", "A", rec.cnt["A"])
                            cq = rec.op("vector", lambda v, h=h: v.tensor_tensor(
                                t2s[:, :], Qarea[:, h * 512:(h + 1) * 512],
                                Qarea[:, h * 512:(h + 1) * 512], ALU.mult), inc="V")
                            rec.wait("tensor", "V", cq)
                            ct = rec.op("tensor", lambda t: t.matmul(
                                bQ, ones_128x1_f32[:, :], t2s[:, :], start=True, stop=True), inc="T")
                            rec.wait("vector", "T", ct)
                            rec.op("vector", lambda v: v.reduce_max(csc[:, 1:2], bQ, axis=AX.X), inc="V")
                            rec.op("vector", lambda v, kh=kh: v.tensor_tensor(
                                csc[:, 2:3], csc[:, 1:2], kn2m[:, kh:kh + 1], ALU.mult), inc="V")
                            cvb = rec.op("vector", lambda v: v.tensor_scalar_mul(
                                csc[:, 3:4], csc[:, 2:3], 1.0 / 128.0), inc="V")
                            rec.wait("scalar", "V", cvb)
                            rec.op("scalar", lambda s: s.activation(
                                csc[:, 4:5], csc[:, 3:4], AF.Ln), inc="A")
                            caa = rec.op("scalar", lambda s: s.activation(
                                csc[:, 5:6], csc[:, 4:5], AF.Exp, scale=0.5), inc="A")
                            rec.wait("vector", "A", caa)
                            cvn = rec.op("vector", lambda v: v.tensor_scalar_mul(
                                csc[:, 6:7], csc[:, 5:6], -1.0), inc="V")
                            rec.wait("tensor", "V", cvn)
                            ctn = rec.op("tensor", lambda t: t.matmul(
                                bN, ones_1x128_f32[:, :], csc[:, 6:7], start=True, stop=True), inc="T")
                            rec.wait("vector", "T", ctn)
                            cvn2 = rec.op("vector", lambda v: v.tensor_copy(negc_col[:, :], bN), inc="V")
                            rec.wait("scalar", "V", cvn2)
                            pexp = {}
                            for jc in range(njc):
                                pp = jc % 2
                                psS = bS[pp]
                                if jc >= 2:
                                    rec.wait("tensor", "A", pexp[jc - 2])
                                rec.op("tensor", lambda t, jc=jc, h=h, kh=kh, psS=psS: t.matmul(
                                    psS, Karea[:, kh * 2048 + jc * 128: kh * 2048 + (jc + 1) * 128],
                                    Qarea[:, h * 512:(h + 1) * 512], start=True, stop=True))
                                cts = rec.bump("tensor")
                                rec.wait("scalar", "T", cts)
                                pexp[jc] = rec.op("scalar", lambda s, jc=jc, pp=pp, psS=psS: s.activation(
                                    p_sb[:, pp * 512:(pp + 1) * 512], psS, AF.Exp,
                                    bias=negc_col[:, :], scale=SCALE), inc="A")
                                delta = jc * 128 - bi
                                if delta >= 128:
                                    di = delta // 128 - 1
                                    rec.wait("vector", "A", pexp[jc])
                                    cvmk = rec.op("vector", lambda v, pp=pp, di=di: v.tensor_tensor(
                                        p_sb[:, pp * 512:(pp + 1) * 512],
                                        p_sb[:, pp * 512:(pp + 1) * 512],
                                        masks_sb[:, di * 512:(di + 1) * 512], ALU.mult), inc="V")
                                    rec.wait("tensor", "V", cvmk)
                                else:
                                    rec.wait("tensor", "A", pexp[jc])
                                rec.op("tensor", lambda t, jc=jc, pp=pp, njc=njc: t.matmul(
                                    bL, ones_128x1_bf[:, :], p_sb[:, pp * 512:(pp + 1) * 512],
                                    start=(jc == 0), stop=(jc == njc - 1)))
                                rec.op("tensor", lambda t, jc=jc, pp=pp, kh=kh, njc=njc: t.matmul(
                                    bO, Varea[:, kh * 2048 + jc * 128: kh * 2048 + (jc + 1) * 128],
                                    p_sb[:, pp * 512:(pp + 1) * 512],
                                    start=(jc == 0), stop=(jc == njc - 1)))
                                rec.bump("tensor")
                            ctl = rec.cnt["T"]
                            rec.wait("vector", "T", ctl)
                            cvr = rec.op("vector", lambda v: v.reciprocal(r_row[:, :], bL), inc="V")
                            rec.wait("tensor", "V", cvr)
                            ctr = rec.op("tensor", lambda t: t.matmul(
                                bR, ones_1x128_f32[:, :], r_row[:, :], start=True, stop=True), inc="T")
                            rec.wait("vector", "T", ctr)
                            rec.op("vector", lambda v: v.tensor_copy(rb_sb[:, :], bR), inc="V")
                            cvo = rec.op("vector", lambda v, h=h: v.tensor_tensor(
                                Tarea[:, h * 512:(h + 1) * 512], bO, rb_sb[:, :], ALU.mult), inc="V")
                            rec.wait("tensor", "V", cvo)
                        for tt in range(4):
                            ti = ib * 4 + tt
                            c_ld = rec.dma("sync", x_tile[:, :], X[ti * 128:(ti + 1) * 128, :])
                            rec.wait("vector", "SD", c_ld)
                            evs = []
                            for fo in range(4):
                                ps = bY[fo % 2]
                                for fc in range(NF):
                                    rec.dma("sync", slab3[:, fc, :],
                                            WO_BF[fo * 512:(fo + 1) * 512, fc * 128:(fc + 1) * 128],
                                            transpose=True)
                                c_wo = rec.cnt["SD"]
                                rec.wait("tensor", "SD", c_wo)
                                if len(evs) >= 2:
                                    rec.wait("tensor", "V", evs[-2])
                                for fc in range(NF):
                                    rec.op("tensor", lambda t, fc=fc, tt=tt, fo=fo, ps=ps: t.matmul(
                                        ps, Tarea[:, fc * 512 + tt * 128: fc * 512 + tt * 128 + 128],
                                        slab3[:, fc, :],
                                        start=(fc == 0), stop=(fc == NF - 1)))
                                ct = rec.bump("tensor")
                                rec.wait("vector", "T", ct)
                                cv = rec.op("vector", lambda v, fo=fo, ps=ps: v.tensor_tensor(
                                    x_new[:, fo * 512:(fo + 1) * 512],
                                    x_tile[:, fo * 512:(fo + 1) * 512],
                                    ps, ALU.add), inc="V")
                                evs.append(cv)
                                rec.wait("sync", "V", cv)
                            rec.wait("gpsimd", "V", evs[-1])
                            c_st = rec.dma("gpsimd", X[ti * 128:(ti + 1) * 128, :], x_new[:, :])
                            rec.wait("vector", "GD", c_st)
                            rec.wait("sync", "V", evs[-1])
                        rec.barrier()
                    rec.barrier()

            rec.barrier()

        if upto >= 2:
            block_L0_ffn()

        def block_L1_ffn():
            wg, wu, wd = inp["L1_ffn_w_gate"], inp["L1_ffn_w_up"], inp["L1_ffn_w_down"]
            sg, su, sd = inp["L1_ffn_s_gate"], inp["L1_ffn_s_up"], inp["L1_ffn_s_down"]
            with ExitStack() as bctx:
                sbl = lambda name, shape, dt: bctx.enter_context(nc.sbuf_tensor(name, shape, dt))
                wg_slab = sbl("wg_slab", [128, 2 * 2048], BF16)
                wu_slab = sbl("wu_slab", [128, 2 * 2048], BF16)
                wd_slab = sbl("wd_slab", [128, 2 * 512], BF16)
                sg_col = sbl("sg_col", [128, 64], F32)
                nsg_col = sbl("nsg_col", [128, 64], F32)
                su_col = sbl("su_col", [128, 64], F32)
                e_scr = sbl("e_scr", [128, 512], F32)
                sig_scr = sbl("sig_scr", [128, 512], F32)
                gate_scr = sbl("gate_scr", [128, 512], F32)
                y_scr = sbl("y_scr", [128, 512], F32)

                c1 = rec.dma("sync", sg_col[:, :], sg.rearrange("(c p) -> p c", p=128),
                             allow_slow_non_contiguous=True)
                c2 = rec.dma("sync", su_col[:, :], su.rearrange("(c p) -> p c", p=128),
                             allow_slow_non_contiguous=True)
                c3 = rec.dma("sync", brow[0:1, 0:2048], sd.rearrange("(o c) -> o c", o=1))
                rec.wait("vector", "SD", max(c1, c2))
                rec.op("vector", lambda v: v.tensor_scalar_mul(nsg_col[:, :], sg_col[:, :], -1.0),
                       inc="V")
                rec.barrier()

                ln_pass(X)

                with nc.psum_tensor("sd_ps", [128, 2048], F32) as sd_ps:
                    rec.wait("tensor", "SD", c3)
                    for j in range(4):
                        rec.op("tensor", lambda t, j=j: t.matmul(
                            sd_ps[:, j * 512:(j + 1) * 512], ones_1x128_f32[:, :],
                            brow[0:1, j * 512:(j + 1) * 512], start=True, stop=True))
                    ct = rec.bump("tensor")
                    rec.wait("vector", "T", ct)
                    rec.op("vector", lambda v: v.tensor_copy(bc_f32[:, :], sd_ps[:, :]), inc="V")
                rec.barrier()

                fuse3 = A2.ap().rearrange("p (ic t) -> p ic t", t=512)
                wg3 = wg_slab.ap().rearrange("p (b fc o) -> p b fc o", b=2, o=128)
                wu3 = wu_slab.ap().rearrange("p (b fc o) -> p b fc o", b=2, o=128)

                with nc.psum_tensor("f_ps", [128, 4096], F32) as FPS:
                    bG = (FPS[:, 0:512], FPS[:, 512:1024])
                    bU = (FPS[:, 1024:1536], FPS[:, 1536:2048])
                    bY = [FPS[:, (4 + t) * 512:(5 + t) * 512] for t in range(4)]
                    for g in range(4):
                        ldg = {}
                        cts_g = {}
                        for oc in range(64):
                            pb = oc % 2
                            if oc == 0:
                                rec.dma("gpsimd", wg3[:, pb, :, :],
                                        wg[:, 0:128].rearrange("(fc p) o -> p fc o", p=128))
                                ldg[0] = rec.dma("gpsimd", wu3[:, pb, :, :],
                                                 wu[:, 0:128].rearrange("(fc p) o -> p fc o", p=128))
                            rec.wait("tensor", "GD", ldg[oc])
                            if oc + 1 < 64:
                                npb = (oc + 1) % 2
                                if oc >= 1:
                                    rec.wait("gpsimd", "T", cts_g[oc - 1])
                                rec.dma("gpsimd", wg3[:, npb, :, :],
                                        wg[:, (oc + 1) * 128:(oc + 2) * 128]
                                        .rearrange("(fc p) o -> p fc o", p=128))
                                ldg[oc + 1] = rec.dma("gpsimd", wu3[:, npb, :, :],
                                                      wu[:, (oc + 1) * 128:(oc + 2) * 128]
                                                      .rearrange("(fc p) o -> p fc o", p=128))
                            psg = bG[pb]
                            psu = bU[pb]
                            for fc in range(NF):
                                rec.op("tensor", lambda t, fc=fc, g=g, pb=pb, psg=psg: t.matmul(
                                    psg, wg3[:, pb, fc, :], XN3[:, fc, g * 512:(g + 1) * 512],
                                    start=(fc == 0), stop=(fc == NF - 1)))
                            for fc in range(NF):
                                rec.op("tensor", lambda t, fc=fc, g=g, pb=pb, psu=psu: t.matmul(
                                    psu, wu3[:, pb, fc, :], XN3[:, fc, g * 512:(g + 1) * 512],
                                    start=(fc == 0), stop=(fc == NF - 1)))
                            ct = rec.bump("tensor")
                            cts_g[oc] = ct
                            rec.wait("scalar", "T", ct)
                            cae = rec.op("scalar", lambda s, oc=oc, psg=psg: s.activation(
                                e_scr[:, :], psg, AF.Exp, scale=nsg_col[:, oc:oc + 1]), inc="A")
                            rec.wait("vector", "A", cae)
                            rec.op("vector", lambda v: v.tensor_scalar_add(sig_scr[:, :], e_scr[:, :], 1.0))
                            rec.op("vector", lambda v: v.reciprocal(sig_scr[:, :], sig_scr[:, :]))
                            rec.op("vector", lambda v, oc=oc, psg=psg: v.scalar_tensor_tensor(
                                gate_scr[:, :], psg, sg_col[:, oc:oc + 1], sig_scr[:, :],
                                ALU.mult, ALU.mult))
                            cvf = rec.op("vector", lambda v, oc=oc, psu=psu: v.scalar_tensor_tensor(
                                fuse3[:, oc, :], psu, su_col[:, oc:oc + 1], gate_scr[:, :],
                                ALU.mult, ALU.mult), inc="V")
                            rec.wait("tensor", "V", cvf)
                        lds = {}
                        cts_d = {}
                        evd = []
                        for fo in range(4):
                            for ic in range(64):
                                pb = ic % 2
                                if ic == 0:
                                    lds[0] = rec.dma("gpsimd", wd_slab[:, 0:512],
                                                     wd[0:128, fo * 512:(fo + 1) * 512])
                                rec.wait("tensor", "GD", lds[ic])
                                if ic + 1 < 64:
                                    npb = (ic + 1) % 2
                                    if ic >= 1:
                                        rec.wait("gpsimd", "T", cts_d[ic - 1])
                                    lds[ic + 1] = rec.dma(
                                        "gpsimd", wd_slab[:, npb * 512:(npb + 1) * 512],
                                        wd[(ic + 1) * 128:(ic + 2) * 128, fo * 512:(fo + 1) * 512])
                                for tt in range(4):
                                    rec.op("tensor", lambda t, ic=ic, tt=tt, pb=pb: t.matmul(
                                        bY[tt], fuse3[:, ic, tt * 128:(tt + 1) * 128],
                                        wd_slab[:, pb * 512:(pb + 1) * 512],
                                        start=(ic == 0), stop=(ic == 63)))
                                cts_d[ic] = rec.bump("tensor")
                            ct = rec.cnt["T"]
                            rec.wait("vector", "T", ct)
                            for tt in range(4):
                                ti = g * 4 + tt
                                c_ld = rec.dma("sync", x_tile[0:128, 0:512],
                                               X[ti * 128:(ti + 1) * 128, fo * 512:(fo + 1) * 512])
                                rec.wait("vector", "SD", c_ld)
                                rec.op("vector", lambda v, tt=tt, fo=fo: v.tensor_tensor(
                                    y_scr[:, :], bY[tt], bc_f32[:, fo * 512:(fo + 1) * 512],
                                    ALU.mult))
                                cv = rec.op("vector", lambda v: v.tensor_tensor(
                                    x_new[:, 0:512], x_tile[0:128, 0:512],
                                    y_scr[:, :], ALU.add), inc="V")
                                rec.wait("gpsimd", "V", cv)
                                c_st = rec.dma("gpsimd",
                                               out_ap[ti * 128:(ti + 1) * 128, fo * 512:(fo + 1) * 512],
                                               x_new[:, 0:512])
                                rec.wait("vector", "GD", c_st)
                                rec.wait("sync", "V", cv)
                            rec.wait("tensor", "V", rec.cnt["V"])
                        rec.barrier()
                rec.barrier()


        if upto >= 3:
            block_L1_attn()
        if upto >= 4:
            block_L1_ffn()

        if not KDUMP and upto < 4:
            cfin = rec.dma("gpsimd", out_ap, X)
            rec.wait("gpsimd", "GD", cfin)
        rec.barrier()

        with nc.Block() as block:
            rec.replay(block)

    return nc



def make_in_maps(inputs, upto=4):
    consts_np = build_consts()
    x = np.asarray(inputs["L0_attn_x"])
    per_core = []
    base = {}
    for k, v in inputs.items():
        if k == "L0_attn_x":
            continue
        base[k] = np.ascontiguousarray(np.asarray(v))
    for name, arr in consts_np.items():
        base[name] = np.ascontiguousarray(arr)
    for c in range(N_CORES):
        m = dict(base)
        m["x_in"] = np.ascontiguousarray(x[c])
        per_core.append(m)
    return per_core


def kernel(**inputs):
    nc = build_program(upto=4)
    in_maps = make_in_maps(inputs)
    res = run_bass_kernel_spmd(nc, in_maps, core_ids=list(range(N_CORES)))
    return np.stack([np.asarray(res.results[c]["out"]) for c in range(N_CORES)], axis=0)


# version 03 (final)
# speedup vs baseline: 1.1032x; 1.1032x over previous
import numpy as np
import ml_dtypes
from contextlib import ExitStack

import os
import concourse.bass as bass
import concourse.mybir as mybir
from concourse.bass_utils import run_bass_kernel_spmd

F32 = mybir.dt.float32
BF16 = mybir.dt.bfloat16
I8 = mybir.dt.int8
U8 = mybir.dt.uint8
AF = mybir.ActivationFunctionType
ALU = mybir.AluOpType
AX = mybir.AxisListType

N_CORES = 8
KPH = int(os.environ.get("KPH", "99"))
KNH = int(os.environ.get("KNH", "16"))
KSV = int(os.environ.get("KSV", "1"))
KDUMP = int(os.environ.get("KDUMP", "0"))
KSERIAL = int(os.environ.get("KSERIAL", "0"))
D = 2048
T = 2048
NT = 16
NF = 16
HD = 128
H0 = 16
CH = 64
NCH = 32
EPS = 1e-5
NEXP = 16
TOPK = 1024
BISECT_ITERS = 38

ENGINES = ("sync", "gpsimd", "vector", "scalar", "tensor")


class Rec:

    def __init__(self, nc):
        self.nc = nc
        self.q = {e: [] for e in ENGINES}
        self.last = {e: None for e in ENGINES}
        self.dirty = {e: False for e in ENGINES}
        self.sem = {}
        self.cnt = {}

    def add_sem(self, name, handle):
        self.sem[name] = handle
        self.cnt[name] = 0

    def op(self, e, fn, inc=None, inc_n=1):
        if inc is not None:
            sem = self.sem[inc]
            def cl(eng, fn=fn, sem=sem, n=inc_n, e=e):
                inst = fn(eng)
                inst.then_inc(sem, n)
                self.last[e] = inst
        else:
            def cl(eng, fn=fn, e=e):
                inst = fn(eng)
                if inst is not None:
                    self.last[e] = inst
        self.q[e].append(cl)
        own = {"tensor": "T", "vector": "V", "scalar": "A", "gpsimd": "G"}.get(e)
        self.dirty[e] = not (inc is not None and inc == own)
        if inc is not None:
            self.cnt[inc] += inc_n
            self.wait(e, inc, self.cnt[inc])
            return self.cnt[inc]
        return None

    def wait(self, e, sem_name, val):
        if val <= 0:
            return
        sem = self.sem[sem_name]
        self.q[e].append(lambda eng, sem=sem, val=val: eng.wait_ge(sem, val))

    def dma(self, queue, out, in_, **kw):
        semname = {"sync": "SD", "gpsimd": "GD", "scalar": "AD"}[queue]
        r = self.op(queue, lambda eng: eng.dma_start(out=out, in_=in_, **kw),
                    inc=semname, inc_n=16)
        self.dirty[queue] = False
        self.last[queue] = None
        return r

    def bump(self, e):
        semname = {"tensor": "T", "vector": "V", "scalar": "A", "gpsimd": "G"}[e]
        sem = self.sem[semname]
        def cl(eng, sem=sem, e=e):
            assert self.last[e] is not None, f"bump on {e} with no prior instruction"
            self.last[e].then_inc(sem, 1)
        self.q[e].append(cl)
        self.cnt[semname] += 1
        self.dirty[e] = False
        self.wait(e, semname, self.cnt[semname])
        return self.cnt[semname]

    def barrier(self):
        for e in ("tensor", "vector", "scalar", "gpsimd"):
            if self.dirty[e]:
                self.bump(e)
                self.dirty[e] = False
        targets = [(s, self.cnt[s]) for s in ("SD", "GD", "AD", "T", "V", "A", "G")]
        skip = {"tensor": {"T"}, "vector": {"V"}, "scalar": {"A", "AD"},
                "gpsimd": {"G", "GD"}, "sync": {"SD"}}
        for e in ENGINES:
            for sname, val in targets:
                if sname in skip[e]:
                    continue
                self.wait(e, sname, val)

    def replay(self, block):
        for e in ENGINES:
            ops = self.q[e]
            getattr(block, e)(lambda eng, ops=ops: [f(eng) for f in ops])



def build_consts():
    c = {}
    c["IDENT_BF"] = np.eye(128, dtype=ml_dtypes.bfloat16)
    c["IDENT_F32"] = np.eye(128, dtype=np.float32)
    i = np.arange(CH, dtype=np.float32)
    difft = (i[None, :] - i[:, None])
    c["DIFFT64"] = np.where(difft >= 0, difft, 0.0).astype(np.float32)
    c["MASKT64"] = (difft >= 0).astype(np.float32)
    c["IOTA64_ROW"] = i[None, :].copy()
    c["REV512_ROW"] = (CH - (np.arange(512) % CH)).astype(ml_dtypes.bfloat16)[None, :].copy()
    c["REV64_COL"] = (CH - i)[:, None].copy()
    c["ONES_1x128_BF"] = np.ones((1, 128), dtype=ml_dtypes.bfloat16)
    c["ONES_1x128_F32"] = np.ones((1, 128), dtype=np.float32)
    c["ONES_1x64_F32"] = np.ones((1, 64), dtype=np.float32)
    c["ONES_128x1_BF"] = np.ones((128, 1), dtype=ml_dtypes.bfloat16)
    c["ONES_128x1_F32"] = np.ones((128, 1), dtype=np.float32)
    p = np.arange(128)
    grp = (p[:, None] % 16 == np.arange(16)[None, :]).astype(np.float32)
    c["GRP"] = grp
    c["GRP2"] = np.ascontiguousarray(grp.T)
    dh = HD
    inv = 1.0 / (10000.0 ** (np.arange(0, dh, 2, dtype=np.float32) / dh))
    tpos = np.arange(T, dtype=np.float32)
    freqs = tpos[None, :] * inv[:, None]
    c["COS_FM"] = np.repeat(np.cos(freqs), 2, axis=0).astype(np.float32)
    c["SIN_FM"] = np.repeat(np.sin(freqs), 2, axis=0).astype(np.float32)
    rot = np.zeros((128, 128), dtype=np.float32)
    for m in range(64):
        rot[m + 64, m] = -1.0
    for m in range(64, 128):
        rot[m - 64, m] = 1.0
    c["ROT_F32"] = rot
    jp = np.arange(128)[:, None]
    ip = np.arange(512)[None, :]
    for delta in (128, 256, 384, 512):
        c[f"MASKT_{delta}"] = ((jp - ip) <= (128 - delta)).astype(ml_dtypes.bfloat16)
    return c



def build_program(upto=4):
    nc = bass.Bass("TRN2", target_bir_lowering=False)
    consts_np = build_consts()

    inp = {}
    def din(name, shape, dt):
        inp[name] = nc.dram_tensor(name, shape, dt, kind="ExternalInput").ap()
    din("x_in", [T, D], F32)
    din("L0_attn_Wqkv", [D, 3 * D], F32)
    din("L0_attn_bqkv", [3 * D], F32)
    din("L0_attn_Wout", [D, D], F32)
    din("L0_attn_bout", [D], F32)
    din("L0_attn_slopes", [H0], F32)
    din("L0_ffn_router_w", [D, NEXP], F32)
    din("L0_ffn_router_b", [NEXP], F32)
    din("L0_ffn_expert_w", [D, D], F32)
    din("L0_ffn_expert_b", [D], F32)
    din("L1_attn_Wq", [D, D], F32)
    din("L1_attn_Wk", [512, D], F32)
    din("L1_attn_Wv", [512, D], F32)
    din("L1_attn_Wo", [D, D], F32)
    din("L1_ffn_w_gate", [D, 4 * D], I8)
    din("L1_ffn_w_up", [D, 4 * D], I8)
    din("L1_ffn_w_down", [4 * D, D], I8)
    din("L1_ffn_s_gate", [4 * D], F32)
    din("L1_ffn_s_up", [4 * D], F32)
    din("L1_ffn_s_down", [D], F32)
    for name, arr in consts_np.items():
        dt = BF16 if arr.dtype == ml_dtypes.bfloat16 else F32
        din(name, list(arr.shape), dt)

    out_ap = nc.dram_tensor("out", [T, D], F32, kind="ExternalOutput").ap()

    X = nc.dram_tensor("X", [T, D], F32, kind="Internal").ap()
    WQ_BF = nc.dram_tensor("WQ_BF", [D, D], BF16, kind="Internal").ap()
    WK_BF = nc.dram_tensor("WK_BF", [512, D], BF16, kind="Internal").ap()
    WV_BF = nc.dram_tensor("WV_BF", [512, D], BF16, kind="Internal").ap()
    WO_BF = nc.dram_tensor("WO_BF", [D, D], BF16, kind="Internal").ap()
    CC_IN = nc.dram_tensor("CC_IN", [NEXP, T], F32, kind="Internal")
    CC_OUT = nc.dram_tensor("CC_OUT", [N_CORES * NEXP, T], F32, kind="Internal")

    with ExitStack() as ctx:
        rec = Rec(nc)
        for s in ("SD", "GD", "AD", "T", "V", "A", "G", "CC"):
            rec.add_sem(s, ctx.enter_context(nc.semaphore(s)))

        sb = lambda name, shape, dt: ctx.enter_context(nc.sbuf_tensor(name, shape, dt))
        A1 = sb("A1", [128, 32768], BF16)
        A2 = sb("A2", [128, 32768], BF16)
        x_tile = sb("x_tile", [128, 2048], F32)
        x_new = sb("x_new", [128, 2048], F32)
        xn_tm = sb("xn_tm", [128, 2048], BF16)
        xsq_scr = sb("xsq_scr", [128, 2048], BF16)
        stash = sb("stash", [128, 64], F32) if KDUMP else None
        stats = sb("stats", [128, 32], F32)
        bc_f32 = sb("bc_f32", [128, 2048], F32)
        brow = sb("brow", [1, 2048], F32)
        ident_bf = sb("ident_bf", [128, 128], BF16)
        ident_f32 = sb("ident_f32", [128, 128], F32)
        ones_1x128_bf = sb("ones_1x128_bf", [1, 128], BF16)
        ones_1x128_f32 = sb("ones_1x128_f32", [1, 128], F32)
        ones_1x64_f32 = sb("ones_1x64_f32", [1, 64], F32)
        ones_128x1_bf = sb("ones_128x1_bf", [128, 1], BF16)
        ones_128x1_f32 = sb("ones_128x1_f32", [128, 1], F32)
        eps_col = sb("eps_col", [128, 1], F32)
        rec.op("vector", lambda v: v.memset(eps_col[:, :], EPS))

        def load_const(name, dest):
            shape = consts_np[name].shape
            return rec.dma("sync", dest[tuple(slice(0, s) for s in shape)], inp[name])

        load_const("IDENT_BF", ident_bf)
        load_const("IDENT_F32", ident_f32)
        load_const("ONES_1x128_BF", ones_1x128_bf)
        load_const("ONES_1x128_F32", ones_1x128_f32)
        load_const("ONES_1x64_F32", ones_1x64_f32)
        load_const("ONES_128x1_BF", ones_128x1_bf)
        load_const("ONES_128x1_F32", ones_128x1_f32)

        if upto >= 3:
            rec.dma("gpsimd", WQ_BF, inp["L1_attn_Wq"])
            rec.dma("gpsimd", WK_BF, inp["L1_attn_Wk"])
            rec.dma("gpsimd", WV_BF, inp["L1_attn_Wv"])
            rec.dma("gpsimd", WO_BF, inp["L1_attn_Wo"])

        XN = A1
        XN3 = XN.ap().rearrange("p (c t) -> p c t", c=16)

        ln_count = [0]

        def ln_pass(src_dram):
            last_ct = [0]
            ln_count[0] += 1
            with nc.psum_tensor(f"ln_ps{ln_count[0]}", [128, 2048], BF16) as ln_ps:
                for i in range(NT):
                    c_ld = rec.dma("sync", x_tile[:, :], src_dram[i * 128:(i + 1) * 128, :])
                    rec.wait("vector", "SD", c_ld)
                    if i >= 1:
                        rec.wait("scalar", "T", last_ct[0])
                    sb_ = (i % 2) * 12
                    S0, S1, S2, S3, S4, SM, SV, SL, SR, SN = (sb_ + j for j in range(10))
                    rec.op("vector", lambda v, S0=S0: v.reduce_sum(
                        stats[:, S0:S0 + 1], x_tile[:, :], axis=AX.X), inc="V")
                    cv0 = rec.op("vector", lambda v, S0=S0, SM=SM: v.tensor_scalar_mul(
                        stats[:, SM:SM + 1], stats[:, S0:S0 + 1], 1.0 / 2048), inc="V")
                    rec.wait("scalar", "V", cv0)
                    ca0 = rec.op("scalar", lambda s, S1=S1: s.activation(
                        xsq_scr[:, :], x_tile[:, :], AF.Square,
                        accum_out=stats[:, S1:S1 + 1]), inc="A")
                    rec.wait("vector", "A", ca0)
                    rec.op("vector", lambda v, S1=S1, S2=S2: v.tensor_scalar_mul(
                        stats[:, S2:S2 + 1], stats[:, S1:S1 + 1], 1.0 / 2048), inc="V")
                    rec.op("vector", lambda v, SM=SM, S3=S3: v.tensor_mul(
                        stats[:, S3:S3 + 1], stats[:, SM:SM + 1], stats[:, SM:SM + 1]), inc="V")
                    cvv1 = rec.op("vector", lambda v, S2=S2, S3=S3, SV=SV: v.tensor_sub(
                        stats[:, SV:SV + 1], stats[:, S2:S2 + 1], stats[:, S3:S3 + 1]), inc="V")
                    rec.wait("scalar", "V", cvv1)
                    rec.op("scalar", lambda s, SV=SV, SL=SL: s.activation(
                        stats[:, SL:SL + 1], stats[:, SV:SV + 1], AF.Ln, bias=eps_col[:, :]), inc="A")
                    ca1 = rec.op("scalar", lambda s, SL=SL, SR=SR: s.activation(
                        stats[:, SR:SR + 1], stats[:, SL:SL + 1], AF.Exp, scale=-0.5), inc="A")
                    rec.wait("vector", "A", ca1)
                    rec.op("vector", lambda v, SM=SM, SR=SR, S4=S4: v.tensor_mul(
                        stats[:, S4:S4 + 1], stats[:, SM:SM + 1], stats[:, SR:SR + 1]), inc="V")
                    rec.op("vector", lambda v, S4=S4, SN=SN: v.tensor_scalar_mul(
                        stats[:, SN:SN + 1], stats[:, S4:S4 + 1], -1.0), inc="V")
                    if KDUMP and i < 2:
                        rec.op("vector", lambda v, i=i: v.tensor_copy(
                            stash[:, i * 32:(i + 1) * 32], stats[:, 0:32]))
                    ca = rec.op("vector", lambda v, SR=SR, SN=SN: v.tensor_scalar(
                        xn_tm[:, :], x_tile[:, :], stats[:, SR:SR + 1], stats[:, SN:SN + 1],
                        ALU.mult, ALU.add), inc="V")
                    rec.wait("tensor", "V", ca)
                    for fc in range(NF):
                        rec.op("tensor", lambda t, fc=fc: t.transpose(
                            ln_ps[:, fc * 128:(fc + 1) * 128],
                            xn_tm[:, fc * 128:(fc + 1) * 128], ident_bf[:, :]))
                    ct = rec.bump("tensor")
                    last_ct[0] = ct
                    rec.wait("vector", "T", ct)
                    cvv = rec.op("vector", lambda v, i=i: v.tensor_copy(
                        XN3[:, :, i * 128:(i + 1) * 128],
                        ln_ps[:, :].rearrange("p (c t) -> p c t", c=16)), inc="V")
                    rec.wait("tensor", "V", cvv)
                    rec.wait("sync", "V", cvv)
                    if KSERIAL:
                        rec.barrier()
            rec.barrier()

        def block_L0_attn():
            Wqkv, bqkv = inp["L0_attn_Wqkv"], inp["L0_attn_bqkv"]
            Wout, bout = inp["L0_attn_Wout"], inp["L0_attn_bout"]
            slopes = inp["L0_attn_slopes"]

            with ExitStack() as bctx:
                sbl = lambda name, shape, dt: bctx.enter_context(nc.sbuf_tensor(name, shape, dt))
                difft = sbl("difft", [64, 64], F32)
                maskt64 = sbl("maskt64", [64, 64], F32)
                iota64r = sbl("iota64r", [1, 64], F32)
                rev64c = sbl("rev64c", [64, 1], F32)
                slopes_sb = sbl("slopes_sb", [16, 1], F32)
                s_row = sbl("s_row", [1, 16], F32)
                negs64 = sbl("negs64", [64, 16], F32)
                decT = sbl("decT", [64, 16 * 64], BF16)
                qdec = sbl("qdec", [128, 16 * 64], BF16)
                kdec = sbl("kdec", [64, 16], F32)
                rev512 = sbl("rev512", [1, 512], BF16)
                krow512 = sbl("krow512", [1, 512], F32)
                kdec_bc = sbl("kdec_bc", [128, 512], F32)
                kdk_fm = sbl("kdk_fm", [128, 2048], BF16)
                bd_col = sbl("bd_col", [128, 16], F32)
                bqkv_cols = sbl("bqkv_cols", [128, 48], F32)
                vbias_bc = sbl("vbias_bc", [128, 128], F32)
                q_fm = sbl("q_fm", [128, 2048], BF16)
                k_fm = sbl("k_fm", [128, 2048], BF16)
                v_st = sbl("v_st", [64, 32 * 128], BF16)
                slab = sbl("slab", [128, 16 * 128], BF16)
                scT_bf = sbl("scT_bf", [64, 2 * 64], BF16)
                kdk = sbl("kdk", [64, 2 * 128], BF16)
                qdq = sbl("qdq", [128, 2 * 64], F32)
                S_f32 = sbl("S_f32", [128, 128], F32)
                qrow = sbl("qrow", [1, 64], F32)

                load_const("DIFFT64", difft)
                load_const("MASKT64", maskt64)
                load_const("IOTA64_ROW", iota64r)
                load_const("REV64_COL", rev64c)
                load_const("REV512_ROW", rev512)
                rec.dma("sync", slopes_sb[:, :], slopes.rearrange("(h o) -> h o", o=1))
                rec.dma("sync", bqkv_cols[:, :], bqkv.rearrange("(c p) -> p c", p=128),
                        allow_slow_non_contiguous=True)
                c_bq = rec.dma("sync", brow[:, :], bqkv.rearrange("(o c) -> o c", o=1)[:, 4096:6144])
                rec.barrier()

                if KPH >= 1:
                  with nc.psum_tensor("dk_ps", [128, 64], F32) as dk_ps, \
                       nc.psum_tensor("dk_ps2", [64, 1], F32) as dk_ps2, \
                       nc.psum_tensor("sr_ps", [1, 16], F32) as sr_ps:
                      ct = rec.op("tensor", lambda t: t.transpose(
                          sr_ps[:, :], slopes_sb[:, :], ident_f32[0:16, 0:16]), inc="T")
                      rec.wait("vector", "T", ct)
                      cv = rec.op("vector", lambda v: v.tensor_copy(s_row[:, :], sr_ps[:, :]), inc="V")
                      rec.wait("tensor", "V", cv)
                      for h in range(H0):
                          ct = rec.op("tensor", lambda t, h=h: t.matmul(
                              dk_ps2[:, :], ones_1x64_f32[:, :], s_row[:, h:h + 1],
                              start=True, stop=True), inc="T")
                          rec.wait("vector", "T", ct)
                          cv = rec.op("vector", lambda v, h=h: v.tensor_scalar(
                              negs64[:, h:h + 1], dk_ps2[:, :], -1.0, None, ALU.mult), inc="V")
                          rec.wait("scalar", "V", cv)
                          rec.wait("tensor", "V", cv)
                          ca = rec.op("scalar", lambda s, h=h: s.activation(
                              decT[:, h * 64:(h + 1) * 64], difft[:, :], AF.Exp,
                              scale=negs64[0:64, h:h + 1]), inc="A")
                          rec.wait("vector", "A", ca)
                          rec.op("vector", lambda v, h=h: v.tensor_tensor(
                              decT[:, h * 64:(h + 1) * 64], decT[:, h * 64:(h + 1) * 64],
                              maskt64[:, :], ALU.mult))
                          rec.op("scalar", lambda s, h=h: s.activation(
                              kdec[:, h:h + 1], rev64c[:, :], AF.Exp,
                              scale=negs64[0:64, h:h + 1]))
                          ca = rec.op("scalar", lambda s, h=h: s.activation(
                              qrow[:, :], iota64r[:, :], AF.Exp,
                              scale=negs64[0:1, h:h + 1]), inc="A")
                          rec.wait("tensor", "A", ca)
                          ct = rec.op("tensor", lambda t, h=h: t.matmul(
                              dk_ps[:, :], ones_1x128_f32[:, :], qrow[:, :],
                              start=True, stop=True), inc="T")
                          rec.wait("vector", "T", ct)
                          cv = rec.op("vector", lambda v, h=h: v.tensor_copy(
                              qdec[:, h * 64:(h + 1) * 64], dk_ps[:, :]), inc="V")
                          rec.wait("tensor", "V", cv)
                          ct = rec.op("tensor", lambda t, h=h: t.matmul(
                              dk_ps[:, 0:1], ones_1x128_f32[:, :], kdec[0:1, h:h + 1],
                              start=True, stop=True), inc="T")
                          rec.wait("vector", "T", ct)
                          cv = rec.op("vector", lambda v, h=h: v.tensor_copy(
                              bd_col[:, h:h + 1], dk_ps[:, 0:1]), inc="V")
                          rec.wait("tensor", "V", cv)
                rec.barrier()

                if KPH >= 2:
                    ln_pass(inp["x_in"])

                attn_fm = A2
                slab3 = slab.ap().rearrange("p (c o) -> p c o", o=128)
                if KPH >= 3:
                  with nc.psum_tensor("ps_arena", [128, 4096], F32) as PSA_h:
                    PSA = PSA_h[:, :]
                    PSB = PSA_h.bitcast(BF16)[:, :]
                    pq_banks = (PSA[:, 0 * 512:1 * 512], PSA[:, 1 * 512:2 * 512])
                    pv_banks = (PSA[:, 2 * 512:2 * 512 + 128], PSA[:, 3 * 512:3 * 512 + 128])
                    sc_banks = (PSA[0:64, 0 * 512:0 * 512 + 64], PSA[0:64, 1 * 512:1 * 512 + 64])
                    ktm_banks = (PSB[0:64, 2 * 1024:2 * 1024 + 128], PSB[0:64, 3 * 1024:3 * 1024 + 128])
                    dl_banks = (PSA[:, 4 * 512:4 * 512 + 128], PSA[:, 5 * 512:5 * 512 + 128])
                    o_banks = (PSA[:, 6 * 512:6 * 512 + 64], PSA[:, 7 * 512:7 * 512 + 64])

                    for h in range(min(H0, KNH)):
                        for which in ("q", "k"):
                            col0 = h * 128 if which == "q" else 2048 + h * 128
                            dst = q_fm if which == "q" else k_fm
                            bcol = h if which == "q" else 16 + h
                            c_sl = rec.dma("gpsimd", slab3,
                                           Wqkv[:, col0:col0 + 128].rearrange("(c p) o -> p c o", p=128))
                            if which == "k":
                                cak = rec.op("scalar", lambda s, h=h: s.activation(
                                    krow512[:, :], rev512[:, :], AF.Exp,
                                    scale=negs64[0:1, h:h + 1]), inc="A")
                                rec.wait("tensor", "A", cak)
                                ctk = rec.op("tensor", lambda t: t.matmul(
                                    pq_banks[0][:, :], ones_1x128_f32[:, :], krow512[:, :],
                                    start=True, stop=True), inc="T")
                                rec.wait("vector", "T", ctk)
                                cvk = rec.op("vector", lambda v: v.tensor_copy(
                                    kdec_bc[:, :], pq_banks[0][:, :]), inc="V")
                                rec.wait("tensor", "V", cvk)
                            rec.wait("tensor", "GD", c_sl)
                            for g in range(4):
                                ps = pq_banks[g % 2]
                                for fc in range(NF):
                                    rec.op("tensor", lambda t, fc=fc, g=g, ps=ps: t.matmul(
                                        ps[:, :], slab3[:, fc, :], XN3[:, fc, g * 512:(g + 1) * 512],
                                        start=(fc == 0), stop=(fc == NF - 1)))
                                ct = rec.bump("tensor")
                                rec.wait("vector", "T", ct)
                                ca = rec.op("vector", lambda v, g=g, ps=ps, dst=dst, bcol=bcol:
                                            v.tensor_scalar(dst[:, g * 512:(g + 1) * 512], ps[:, :],
                                                            bqkv_cols[:, bcol:bcol + 1], None,
                                                            ALU.add), inc="V")
                                if which == "k":
                                    ca = rec.op("vector", lambda v, g=g, ps=ps, bcol=bcol:
                                                v.scalar_tensor_tensor(
                                                    kdk_fm[:, g * 512:(g + 1) * 512], ps[:, :],
                                                    bqkv_cols[:, bcol:bcol + 1], kdec_bc[:, :],
                                                    ALU.add, ALU.mult), inc="V")
                                if g >= 1:
                                    rec.wait("tensor", "V", ca - 1)
                            rec.wait("tensor", "V", rec.cnt["V"])
                            rec.wait("gpsimd", "V", rec.cnt["V"])
                        if not KSV:
                            continue
                        c_sl = rec.dma("gpsimd", slab3,
                                       Wqkv[:, 4096 + h * 128:4096 + (h + 1) * 128]
                                       .rearrange("(c p) o -> p c o", p=128))
                        rec.wait("tensor", "SD", c_bq)
                        ct = rec.op("tensor", lambda t, h=h: t.matmul(
                            pv_banks[1][:, :], ones_1x128_f32[:, :],
                            brow[:, h * 128:(h + 1) * 128],
                            start=True, stop=True), inc="T")
                        rec.wait("vector", "T", ct)
                        cv = rec.op("vector", lambda v: v.tensor_copy(vbias_bc[:, :], pv_banks[1][:, :]),
                                    inc="V")
                        rec.wait("tensor", "V", cv)
                        rec.wait("tensor", "GD", c_sl)
                        for i in range(NT):
                            ps = pv_banks[i % 2]
                            for fc in range(NF):
                                rec.op("tensor", lambda t, fc=fc, i=i, ps=ps: t.matmul(
                                    ps[:, :], XN3[:, fc, i * 128:(i + 1) * 128], slab3[:, fc, :],
                                    start=(fc == 0), stop=(fc == NF - 1)))
                            ct = rec.bump("tensor")
                            rec.wait("vector", "T", ct)
                            rec.op("vector", lambda v, i=i, ps=ps: v.tensor_tensor(
                                v_st[0:64, (2 * i) * 128:(2 * i + 1) * 128],
                                ps[0:64, :], vbias_bc[0:64, :], ALU.add))
                            cv = rec.op("vector", lambda v, i=i, ps=ps: v.tensor_tensor(
                                v_st[0:64, (2 * i + 1) * 128:(2 * i + 2) * 128],
                                ps[64:128, :], vbias_bc[64:128, :], ALU.add), inc="V")
                            if i >= 1:
                                rec.wait("tensor", "V", cv - 1)
                        cv_last = rec.cnt["V"]
                        rec.wait("tensor", "V", cv_last)
                        rec.wait("gpsimd", "V", cv_last)

                        if KPH < 4:
                            continue
                        dT = decT[:, h * 64:(h + 1) * 64]
                        qd = qdec[:, h * 64:(h + 1) * 64]
                        kd = kdec[:, h:h + 1]
                        bd = bd_col[:, h:h + 1]
                        cv0 = rec.op("vector", lambda v: v.memset(S_f32[:, :], 0.0), inc="V")
                        rec.wait("tensor", "V", cv0)
                        stt = {}
                        oev = {}
                        dvk = {}
                        tdl = {}
                        tsc = {}
                        to4 = {}
                        for c in range(NCH):
                            p = c % 2
                            if c >= 2:
                                rec.wait("tensor", "V", dvk[c - 2])
                            rec.op("tensor", lambda t, c=c, p=p: t.matmul(
                                sc_banks[p], k_fm[:, c * 64:(c + 1) * 64],
                                q_fm[:, c * 64:(c + 1) * 64], start=True, stop=True))
                            rec.op("tensor", lambda t, c=c, p=p: t.transpose(
                                ktm_banks[p], kdk_fm[:, c * 64:(c + 1) * 64],
                                ident_bf[:, :]))
                            tsc[c] = rec.bump("tensor")
                            rec.wait("vector", "T", tsc[c])
                            rec.op("vector", lambda v, c=c, p=p, dT=dT: v.tensor_tensor(
                                scT_bf[:, p * 64:(p + 1) * 64], sc_banks[p],
                                dT, ALU.mult))
                            rec.op("vector", lambda v, c=c, p=p: v.tensor_copy(
                                kdk[:, p * 128:(p + 1) * 128], ktm_banks[p]))
                            dvk[c] = rec.op("vector", lambda v, c=c, p=p, qd=qd: v.tensor_tensor(
                                qdq[:, p * 64:(p + 1) * 64], q_fm[:, c * 64:(c + 1) * 64],
                                qd, ALU.mult), inc="V")
                            rec.wait("tensor", "V", dvk[c])
                            if c >= 2:
                                rec.wait("tensor", "V", stt[c - 2])
                            rec.op("tensor", lambda t, c=c, p=p: t.matmul(
                                dl_banks[p], kdk[:, p * 128:(p + 1) * 128],
                                v_st[0:64, c * 128:(c + 1) * 128], start=True, stop=True))
                            tdl[c] = rec.bump("tensor")
                            if c >= 2:
                                rec.wait("tensor", "A", oev[c - 2])
                            if c >= 1:
                                rec.wait("tensor", "V", stt[c - 1])
                            rec.op("tensor", lambda t, c=c, p=p: t.matmul(
                                o_banks[p], v_st[0:64, c * 128:(c + 1) * 128],
                                scT_bf[0:64, p * 64:(p + 1) * 64], start=True, stop=(c == 0)))
                            if c >= 1:
                                rec.op("tensor", lambda t, c=c, p=p: t.matmul(
                                    o_banks[p], S_f32[:, :],
                                    qdq[:, p * 64:(p + 1) * 64], start=False, stop=True))
                            to4[c] = rec.bump("tensor")
                            rec.wait("vector", "T", to4[c])
                            stt[c] = rec.op("vector", lambda v, c=c, p=p, bd=bd: v.scalar_tensor_tensor(
                                S_f32[:, :], S_f32[:, :], bd, dl_banks[p],
                                ALU.mult, ALU.add), inc="V")
                            rec.wait("scalar", "T", to4[c])
                            oev[c] = rec.op("scalar", lambda s, c=c, p=p, h=h: s.copy(
                                attn_fm[:, h * 2048 + c * 64: h * 2048 + (c + 1) * 64],
                                o_banks[p]), inc="A")
                        rec.wait("tensor", "V", rec.cnt["V"])
                        rec.wait("tensor", "A", rec.cnt["A"])
                        if KDUMP == 5 and h == 0:
                            cvd = rec.op("vector", lambda v: v.tensor_copy(
                                x_new[:, :], attn_fm.ap()[:, 0:2048]), inc="V")
                            rec.wait("sync", "V", cvd)
                            cds = rec.dma("sync", out_ap[1792:1920, :], x_new[:, :])
                            rec.wait("vector", "SD", cds)
                            rec.barrier()
                rec.barrier()

                if KDUMP == 2:
                    X16d = XN.ap().rearrange("p (a u) -> p a u", a=16)
                    for c in range(8):
                        cvd = rec.op("vector", lambda v, c=c: v.tensor_copy(
                            x_new[:, :], X16d[:, c, :]), inc="V")
                        rec.wait("sync", "V", cvd)
                        cds = rec.dma("sync", out_ap[c * 128:(c + 1) * 128, :], x_new[:, :])
                        rec.wait("vector", "SD", cds)
                    rec.barrier()
                if KDUMP == 5:
                    for hh in range(2):
                        cvd = rec.op("vector", lambda v, hh=hh: v.tensor_copy(
                            x_new[:, :], attn_fm.ap()[:, hh * 2048:(hh + 1) * 2048]), inc="V")
                        rec.wait("sync", "V", cvd)
                        cds = rec.dma("sync", out_ap[hh * 128:(hh + 1) * 128, :], x_new[:, :])
                        rec.wait("vector", "SD", cds)
                    rec.barrier()
                if KDUMP == 4:
                    def sd4(row0, ap_, rows=64):
                        cvd = rec.op("vector", lambda v, a=ap_, r=rows: v.tensor_copy(x_new[0:r, 0:a.shape[-1]], a), inc="V")
                        rec.wait("sync", "V", cvd)
                        cds = rec.dma("sync", out_ap[row0:row0 + rows, 0:ap_.shape[-1]], x_new[0:rows, 0:ap_.shape[-1]])
                        rec.wait("vector", "SD", cds)
                    sd4(0, decT[:, :], 64)
                    sd4(64, qdec.ap()[:, :], 128)
                    sd4(192, kdec[:, :], 64)
                    sd4(256, bd_col[:, :], 128)
                    sd4(384, negs64[:, :], 64)
                    sd4(448, s_row[:, :], 1)
                    rec.barrier()
                if KDUMP == 3:
                    for hh in range(16):
                        cvd = rec.op("vector", lambda v, hh=hh: v.tensor_copy(
                            x_new[:, :], attn_fm.ap()[:, hh * 2048:(hh + 1) * 2048]), inc="V")
                        rec.wait("sync", "V", cvd)
                        cds = rec.dma("sync", out_ap[hh * 128:(hh + 1) * 128, :], x_new[:, :])
                        rec.wait("vector", "SD", cds)
                    rec.barrier()
                if KDUMP == 1:
                    def safe_dump(row0, src_ap):
                        cvd = rec.op("vector", lambda v, s=src_ap: v.tensor_copy(x_new[:, :], s), inc="V")
                        rec.wait("sync", "V", cvd)
                        cds = rec.dma("sync", out_ap[row0:row0 + 128, :], x_new[:, :])
                        rec.wait("vector", "SD", cds)
                    safe_dump(0 * 128, q_fm[:, :])
                    safe_dump(1 * 128, k_fm[:, :])
                    cvd = rec.op("vector", lambda v: v.tensor_copy(x_new[0:64, :], v_st[0:64, 0:2048]), inc="V")
                    rec.wait("sync", "V", cvd)
                    cds = rec.dma("sync", out_ap[2 * 128:2 * 128 + 64, :], x_new[0:64, :])
                    rec.wait("vector", "SD", cds)
                    cvd = rec.op("vector", lambda v: v.tensor_copy(x_new[0:64, :], v_st[0:64, 2048:4096]), inc="V")
                    rec.wait("sync", "V", cvd)
                    cds = rec.dma("sync", out_ap[2 * 128 + 64:3 * 128, :], x_new[0:64, :])
                    rec.wait("vector", "SD", cds)
                    safe_dump(3 * 128, attn_fm.ap()[:, 15 * 2048:16 * 2048])
                    rec.barrier()

                if KPH < 5:
                    return
                W_res = A1
                W3 = W_res.ap().rearrange("p (c o) -> p c o", c=16)
                c_w = rec.dma("gpsimd", W3, Wout.rearrange("(c p) o -> p c o", p=128))
                rec.wait("tensor", "GD", c_w)
                c_br = rec.dma("sync", brow[0:1, 0:2048], bout.rearrange("(o c) -> o c", o=1))
                with nc.psum_tensor("bb_ps", [128, 2048], F32) as bb_ps:
                    rec.wait("tensor", "SD", c_br)
                    for j in range(4):
                        rec.op("tensor", lambda t, j=j: t.matmul(
                            bb_ps[:, j * 512:(j + 1) * 512], ones_1x128_f32[:, :],
                            brow[0:1, j * 512:(j + 1) * 512], start=True, stop=True))
                    ct = rec.bump("tensor")
                    rec.wait("vector", "T", ct)
                    cv = rec.op("vector", lambda v: v.tensor_copy(bc_f32[:, :], bb_ps[:, :]), inc="V")
                rec.barrier()

                A23 = attn_fm.ap().rearrange("p (hh t) -> p hh t", hh=16)
                with nc.psum_tensor("op_ps", [128, 512], F32) as op_ps, \
                     nc.psum_tensor("op_ps2", [128, 512], F32) as op_ps2:
                    evs = []
                    for i in range(NT):
                        c_ld = rec.dma("sync", x_tile[:, :], inp["x_in"][i * 128:(i + 1) * 128, :])
                        rec.wait("vector", "SD", c_ld)
                        cxb = rec.op("vector", lambda v: v.tensor_tensor(
                            x_new[:, :], x_tile[:, :], bc_f32[:, :], ALU.add), inc="V")
                        for fo in range(4):
                            ps = op_ps if fo % 2 == 0 else op_ps2
                            if len(evs) >= 2:
                                rec.wait("tensor", "V", evs[-2])
                            for fi in range(NF):
                                rec.op("tensor", lambda t, fi=fi, fo=fo, i=i, ps=ps: t.matmul(
                                    ps[:, :], A23[:, fi, i * 128:(i + 1) * 128],
                                    W3[:, fi, fo * 512:(fo + 1) * 512],
                                    start=(fi == 0), stop=(fi == NF - 1)))
                            ct = rec.bump("tensor")
                            rec.wait("vector", "T", ct)
                            cv = rec.op("vector", lambda v, fo=fo, ps=ps: v.tensor_tensor(
                                x_new[:, fo * 512:(fo + 1) * 512], x_new[:, fo * 512:(fo + 1) * 512],
                                ps[:, :], ALU.add), inc="V")
                            evs.append(cv)
                        rec.wait("gpsimd", "V", evs[-1])
                        c_st = rec.dma("gpsimd", X[i * 128:(i + 1) * 128, :], x_new[:, :])
                        rec.wait("vector", "GD", c_st)
                        rec.wait("sync", "V", cxb)
                rec.barrier()


        def block_L0_ffn():
            rw, rb = inp["L0_ffn_router_w"], inp["L0_ffn_router_b"]
            ew, eb = inp["L0_ffn_expert_w"], inp["L0_ffn_expert_b"]

            with ExitStack() as bctx:
                sbl = lambda name, shape, dt: bctx.enter_context(nc.sbuf_tensor(name, shape, dt))
                rw_sb = sbl("rw_sb", [128, 16 * 16], BF16)
                rb_bc = sbl("rb_bc", [128, 16], F32)
                S_local = sbl("S_local", [128, 16 * 16], F32)
                S_rowT = sbl("S_rowT", [16, 2048], F32)
                S_big = sbl("S_big", [128, 2048], F32)
                grp_sb = sbl("grp_sb", [128, 16], F32)
                grp2_sb = sbl("grp2_sb", [16, 128], F32)
                bis = sbl("bis", [128, 8], F32)
                cnt16_sb = sbl("cnt16_sb", [16, 1], F32)
                updm = sbl("updm", [128, 1], mybir.dt.uint8)
                thr_row = sbl("thr_row", [1, 16], F32)
                thr_bc = sbl("thr_bc", [128, 16], F32)
                gsum_all = sbl("gsum_all", [128, 16], F32)
                gscr = sbl("gscr", [128, 16], F32)
                t1_scr = sbl("t1_scr", [128, 512], F32)

                c_rw = rec.dma("gpsimd", rw_sb[:, :].rearrange("p (c e) -> p c e", e=16),
                               rw.rearrange("(c p) e -> p c e", p=128))
                c_rb = rec.dma("sync", brow[0:1, 0:16], rb.rearrange("(o c) -> o c", o=1))
                load_const("GRP", grp_sb)
                load_const("GRP2", grp2_sb)
                rec.barrier()

                ln_pass(X)
                rw3 = rw_sb.ap().rearrange("p (c e) -> p c e", e=16)

                with nc.psum_tensor("r_psA", [128, 4096], F32) as r_psA:
                    r_ps = (r_psA[:, 0:16], r_psA[:, 512:528])
                    bb_ps = r_psA[:, 1024:1040]
                    rec.wait("tensor", "SD", c_rb)
                    ct = rec.op("tensor", lambda t: t.matmul(
                        bb_ps, ones_1x128_f32[:, :], brow[0:1, 0:16],
                        start=True, stop=True), inc="T")
                    rec.wait("vector", "T", ct)
                    rec.op("vector", lambda v: v.tensor_copy(rb_bc[:, :], bb_ps), inc="V")
                    evr = {}
                    for i in range(NT):
                        ps = r_ps[i % 2]
                        if i >= 2:
                            rec.wait("tensor", "V", evr[i - 2])
                        for fc in range(NF):
                            rec.op("tensor", lambda t, fc=fc, i=i, ps=ps: t.matmul(
                                ps, XN3[:, fc, i * 128:(i + 1) * 128], rw3[:, fc, :],
                                start=(fc == 0), stop=(fc == NF - 1)))
                        ct = rec.bump("tensor")
                        rec.wait("vector", "T", ct)
                        rec.op("vector", lambda v, i=i, ps=ps: v.tensor_tensor(
                            gscr[:, :], ps, rb_bc[:, :], ALU.add), inc="V")
                        rec.op("vector", lambda v: v.reduce_max(
                            bis[:, 5:6], gscr[:, :], axis=AX.X), inc="V")
                        cvm = rec.op("vector", lambda v: v.tensor_scalar_mul(
                            bis[:, 6:7], bis[:, 5:6], -1.0), inc="V")
                        rec.wait("scalar", "V", cvm)
                        cae = rec.op("scalar", lambda s: s.activation(
                            S_local[0:128, 0:16] if False else gscr[:, :], gscr[:, :], AF.Exp,
                            bias=bis[:, 6:7], accum_out=bis[:, 7:8]), inc="A")
                        rec.wait("vector", "A", cae)
                        rec.op("vector", lambda v: v.reciprocal(bis[:, 4:5], bis[:, 7:8]), inc="V")
                        evr[i] = rec.op("vector", lambda v, i=i: v.tensor_scalar(
                            S_local[:, i * 16:(i + 1) * 16], gscr[:, :], bis[:, 4:5], None,
                            ALU.mult), inc="V")
                    tr_ps = (r_psA[:, 1536:1664], r_psA[:, 2048:2176])
                    evt = {}
                    for i in range(NT):
                        ps = tr_ps[i % 2]
                        if i >= 2:
                            rec.wait("tensor", "V", evt[i - 2])
                        rec.op("tensor", lambda t, i=i, ps=ps: t.transpose(
                            ps[0:16, :], S_local[:, i * 16:(i + 1) * 16], ident_f32[:, :]))
                        ct = rec.bump("tensor")
                        rec.wait("vector", "T", ct)
                        evt[i] = rec.op("vector", lambda v, i=i, ps=ps: v.tensor_copy(
                            S_rowT[:, i * 128:(i + 1) * 128], ps[0:16, :]), inc="V")
                    cvl = rec.cnt["V"]
                    rec.wait("gpsimd", "V", cvl)
                    c_ci = rec.dma("gpsimd", CC_IN.ap(), S_rowT[:, :])
                    rec.wait("gpsimd", "GD", c_ci)
                    rec.op("gpsimd", lambda g: g.collective_compute(
                        "AllGather", ALU.bypass,
                        replica_groups=[list(range(N_CORES))],
                        ins=[CC_IN.ap().opt()],
                        outs=[CC_OUT.ap().opt()]), inc="CC", inc_n=1)
                    rec.wait("gpsimd", "CC", 1)
                    c_sb = rec.dma("gpsimd", S_big[:, :], CC_OUT.ap())
                    rec.wait("vector", "GD", c_sb)
                    rec.wait("tensor", "GD", c_sb)

                    b_ps16 = r_psA[:, 2560:2561]
                    b_ps128 = r_psA[:, 3072:3073]
                    rec.op("vector", lambda v: v.memset(bis[:, 0:1], 0.0), inc="V")
                    cvh = rec.op("vector", lambda v: v.memset(bis[:, 1:2], 1.0), inc="V")
                    for it in range(BISECT_ITERS):
                        rec.op("vector", lambda v: v.tensor_scalar(
                            bis[:, 2:3], bis[:, 0:1], bis[:, 1:2], 0.5,
                            ALU.add, ALU.mult), inc="V")
                        cvc = rec.op("vector", lambda v: v.tensor_scalar(
                            xsq_scr[:, :], S_big[:, :], bis[:, 2:3], 0.0,
                            ALU.is_ge, ALU.add, accum_out=bis[:, 3:4]), inc="V")
                        rec.wait("tensor", "V", cvc)
                        ct1 = rec.op("tensor", lambda t: t.matmul(
                            b_ps16[0:16, :], grp_sb[:, :], bis[:, 3:4],
                            start=True, stop=True), inc="T")
                        rec.wait("vector", "T", ct1)
                        cv16 = rec.op("vector", lambda v: v.tensor_copy(
                            cnt16_sb[:, :], b_ps16[0:16, :]), inc="V")
                        rec.wait("tensor", "V", cv16)
                        ct2 = rec.op("tensor", lambda t: t.matmul(
                            b_ps128, grp2_sb[:, :], cnt16_sb[:, :],
                            start=True, stop=True), inc="T")
                        rec.wait("vector", "T", ct2)
                        rec.op("vector", lambda v: v.tensor_scalar(
                            updm[:, :], b_ps128, float(TOPK), None, ALU.is_ge), inc="V")
                        rec.op("vector", lambda v: v.select(
                            bis[:, 0:1], updm[:, :], bis[:, 2:3], bis[:, 0:1]), inc="V")
                        rec.op("vector", lambda v: v.select(
                            bis[:, 1:2], updm[:, :], bis[:, 1:2], bis[:, 2:3]), inc="V")
                    cvt = rec.cnt["V"]
                    rec.wait("tensor", "V", cvt)
                    ct = rec.op("tensor", lambda t: t.transpose(
                        b_ps16[0:1, 0:16] if False else r_psA[0:1, 2560:2576],
                        bis[0:16, 0:1], ident_f32[0:16, 0:16]), inc="T")
                    rec.wait("vector", "T", ct)
                    cvr = rec.op("vector", lambda v: v.tensor_copy(
                        thr_row[:, :], r_psA[0:1, 2560:2576]), inc="V")
                    rec.wait("tensor", "V", cvr)
                    ct = rec.op("tensor", lambda t: t.matmul(
                        r_psA[:, 3584:3600], ones_1x128_f32[:, :], thr_row[:, :],
                        start=True, stop=True), inc="T")
                    rec.wait("vector", "T", ct)
                    rec.op("vector", lambda v: v.tensor_copy(thr_bc[:, :], r_psA[:, 3584:3600]), inc="V")
                    for i in range(NT):
                        rec.op("vector", lambda v, i=i: v.tensor_tensor(
                            gscr[:, :], S_local[:, i * 16:(i + 1) * 16], thr_bc[:, :], ALU.is_ge), inc="V")
                        rec.op("vector", lambda v, i=i: v.tensor_tensor(
                            gscr[:, :], gscr[:, :], S_local[:, i * 16:(i + 1) * 16], ALU.mult), inc="V")
                        rec.op("vector", lambda v, i=i: v.reduce_sum(
                            gsum_all[:, i:i + 1], gscr[:, :], axis=AX.X), inc="V")
                rec.barrier()

                W_res = A2
                W3e = W_res.ap().rearrange("p (c o) -> p c o", c=16)
                c_w = rec.dma("gpsimd", W3e, ew.rearrange("(c p) o -> p c o", p=128))
                rec.wait("tensor", "GD", c_w)
                c_br = rec.dma("sync", brow[0:1, 0:2048], eb.rearrange("(o c) -> o c", o=1))
                with nc.psum_tensor("eb_ps", [128, 2048], F32) as eb_ps:
                    rec.wait("tensor", "SD", c_br)
                    for j in range(4):
                        rec.op("tensor", lambda t, j=j: t.matmul(
                            eb_ps[:, j * 512:(j + 1) * 512], ones_1x128_f32[:, :],
                            brow[0:1, j * 512:(j + 1) * 512], start=True, stop=True))
                    ct = rec.bump("tensor")
                    rec.wait("vector", "T", ct)
                    rec.op("vector", lambda v: v.tensor_copy(bc_f32[:, :], eb_ps[:, :]), inc="V")
                rec.barrier()

                with nc.psum_tensor("e_ps", [128, 1024], F32) as e_ps:
                    ep = (e_ps[:, 0:512], e_ps[:, 512:1024])
                    evs = []
                    for i in range(NT):
                        c_ld = rec.dma("sync", x_tile[:, :], X[i * 128:(i + 1) * 128, :])
                        rec.wait("vector", "SD", c_ld)
                        for fo in range(4):
                            ps = ep[fo % 2]
                            if len(evs) >= 2:
                                rec.wait("tensor", "V", evs[-2])
                            for fc in range(NF):
                                rec.op("tensor", lambda t, fc=fc, fo=fo, i=i, ps=ps: t.matmul(
                                    ps, XN3[:, fc, i * 128:(i + 1) * 128],
                                    W3e[:, fc, fo * 512:(fo + 1) * 512],
                                    start=(fc == 0), stop=(fc == NF - 1)))
                            ct = rec.bump("tensor")
                            rec.wait("vector", "T", ct)
                            rec.op("vector", lambda v, fo=fo, ps=ps: v.tensor_tensor(
                                t1_scr[:, :], ps, bc_f32[:, fo * 512:(fo + 1) * 512], ALU.add))
                            cv = rec.op("vector", lambda v, fo=fo, i=i: v.scalar_tensor_tensor(
                                x_new[:, fo * 512:(fo + 1) * 512], t1_scr[:, :],
                                gsum_all[:, i:i + 1], x_tile[:, fo * 512:(fo + 1) * 512],
                                ALU.mult, ALU.add), inc="V")
                            evs.append(cv)
                        rec.wait("gpsimd", "V", evs[-1])
                        c_st = rec.dma("gpsimd", X[i * 128:(i + 1) * 128, :], x_new[:, :])
                        rec.wait("vector", "GD", c_st)
                        rec.wait("sync", "V", evs[-1])
                rec.barrier()

        if upto >= 1:
            block_L0_attn()

        def block_L1_attn():
            H1, KVH = 16, 4
            SCALE = float(HD) ** -0.5
            with ExitStack() as bctx:
                sbl = lambda name, shape, dt: bctx.enter_context(nc.sbuf_tensor(name, shape, dt))
                slab = sbl("l1slab", [128, 16 * 512], BF16)
                cos_s = sbl("cos_s", [128, 512], F32)
                sin_s = sbl("sin_s", [128, 512], F32)
                rot_sb = sbl("rot_sb", [128, 128], F32)
                masks_sb = sbl("masks_sb", [128, 4 * 512], BF16)
                qpre = sbl("qpre", [128, 512], F32)
                t2s = sbl("t2s", [128, 512], F32)
                t3s = sbl("t3s", [128, 512], F32)
                p_sb = sbl("p_sb", [128, 2 * 512], BF16)
                rb_sb = sbl("rb_sb", [128, 512], F32)
                r_row = sbl("r_row", [1, 512], F32)
                kn2m = sbl("kn2m", [1, 8], F32)
                csc = sbl("csc", [1, 8], F32)
                negc_col = sbl("negc_col", [128, 1], F32)

                load_const("ROT_F32", rot_sb)
                for di, delta in enumerate((128, 256, 384, 512)):
                    load_const(f"MASKT_{delta}", masks_sb[:, di * 512:(di + 1) * 512]) if False else None
                    rec.dma("sync", masks_sb[:, di * 512:(di + 1) * 512], inp[f"MASKT_{delta}"])
                rec.barrier()

                ln_pass(X)

                Karea = A2.ap()[:, 0:8192]
                Varea = A2.ap()[:, 8192:16384]
                Qarea = A2.ap()[:, 16384:24576]
                Tarea = A2.ap()[:, 24576:32768]
                slab3 = slab.ap().rearrange("p (c o) -> p c o", o=512)

                with nc.psum_tensor("l1_ps", [128, 4096], F32) as PS:
                    bS = (PS[:, 0:512], PS[:, 512:1024])
                    bO = PS[:, 1024:1536]
                    bL = PS[0:1, 1536:2048]
                    bQ = PS[0:1, 1536 + 0:1536 + 512]
                    bR = PS[:, 2048:2560]
                    bN = PS[:, 2560:2561]
                    bY = (PS[:, 3072:3584], PS[:, 3584:4096])

                    for fc in range(NF):
                        rec.dma("sync", slab3[:, fc, :], WK_BF[0:512, fc * 128:(fc + 1) * 128],
                                transpose=True)
                    c_wk = rec.cnt["SD"]
                    rec.wait("tensor", "SD", c_wk)
                    for g in range(4):
                        c_cs = rec.dma("sync", cos_s[:, :], inp["COS_FM"][:, g * 512:(g + 1) * 512])
                        c_sn = rec.dma("sync", sin_s[:, :], inp["SIN_FM"][:, g * 512:(g + 1) * 512])
                        rec.wait("vector", "SD", c_sn)
                        for kh in range(KVH):
                            ps = bS[kh % 2]
                            for fc in range(NF):
                                rec.op("tensor", lambda t, fc=fc, g=g, kh=kh, ps=ps: t.matmul(
                                    ps, slab3[:, fc, kh * 128:(kh + 1) * 128],
                                    XN3[:, fc, g * 512:(g + 1) * 512],
                                    start=(fc == 0), stop=(fc == NF - 1)))
                            ct = rec.bump("tensor")
                            rec.wait("vector", "T", ct)
                            cv = rec.op("vector", lambda v, ps=ps: v.tensor_copy(qpre[:, :], ps), inc="V")
                            rec.wait("tensor", "V", cv)
                            ct2 = rec.op("tensor", lambda t: t.matmul(
                                bR, rot_sb[:, :], qpre[:, :], start=True, stop=True), inc="T")
                            rec.wait("vector", "T", ct2)
                            rec.op("vector", lambda v: v.tensor_tensor(
                                t2s[:, :], bR, sin_s[:, :], ALU.mult))
                            rec.op("vector", lambda v: v.tensor_tensor(
                                t3s[:, :], qpre[:, :], cos_s[:, :], ALU.mult))
                            cv2 = rec.op("vector", lambda v, kh=kh, g=g: v.tensor_tensor(
                                Karea[:, kh * 2048 + g * 512: kh * 2048 + (g + 1) * 512],
                                t3s[:, :], t2s[:, :], ALU.add), inc="V")
                            rec.wait("tensor", "V", cv2)
                        rec.wait("sync", "V", rec.cnt["V"])
                    for fc in range(NF):
                        rec.dma("sync", slab3[:, fc, :], WV_BF[0:512, fc * 128:(fc + 1) * 128],
                                transpose=True)
                    c_wv = rec.cnt["SD"]
                    rec.wait("tensor", "SD", c_wv)
                    Vv = Varea.rearrange("p (kh tt d) -> p kh tt d", kh=4, d=128)
                    for tt in range(NT):
                        ps = bS[tt % 2]
                        for fc in range(NF):
                            rec.op("tensor", lambda t, fc=fc, tt=tt, ps=ps: t.matmul(
                                ps, XN3[:, fc, tt * 128:(tt + 1) * 128], slab3[:, fc, :],
                                start=(fc == 0), stop=(fc == NF - 1)))
                        ct = rec.bump("tensor")
                        rec.wait("vector", "T", ct)
                        cv = rec.op("vector", lambda v, tt=tt, ps=ps: v.tensor_copy(
                            Vv[:, :, tt, :], ps.rearrange("p (kh d) -> p kh d", d=128)), inc="V")
                        rec.wait("tensor", "V", cv)
                    cvz = rec.op("vector", lambda v: v.memset(kn2m[:, :], 0.0), inc="V")
                    for kh in range(KVH):
                        for g in range(4):
                            cv = rec.op("vector", lambda v, kh=kh, g=g: v.tensor_tensor(
                                t2s[:, :], Karea[:, kh * 2048 + g * 512: kh * 2048 + (g + 1) * 512],
                                Karea[:, kh * 2048 + g * 512: kh * 2048 + (g + 1) * 512],
                                ALU.mult), inc="V")
                            rec.wait("tensor", "V", cv)
                            ct = rec.op("tensor", lambda t: t.matmul(
                                bQ, ones_128x1_f32[:, :], t2s[:, :], start=True, stop=True), inc="T")
                            rec.wait("vector", "T", ct)
                            rec.op("vector", lambda v: v.reduce_max(
                                csc[:, 0:1], bQ, axis=AX.X), inc="V")
                            rec.op("vector", lambda v, kh=kh: v.tensor_tensor(
                                kn2m[:, kh:kh + 1], kn2m[:, kh:kh + 1], csc[:, 0:1], ALU.max), inc="V")
                    rec.barrier()

                    oev_prev = [0]
                    for ib in range(4):
                        bi = ib * 512
                        c_cs = rec.dma("sync", cos_s[:, :], inp["COS_FM"][:, bi:bi + 512])
                        c_sn = rec.dma("sync", sin_s[:, :], inp["SIN_FM"][:, bi:bi + 512])
                        rec.wait("vector", "SD", c_sn)
                        for hg in range(4):
                            for fc in range(NF):
                                rec.dma("sync", slab3[:, fc, :],
                                        WQ_BF[hg * 512:(hg + 1) * 512, fc * 128:(fc + 1) * 128],
                                        transpose=True)
                            c_wq = rec.cnt["SD"]
                            rec.wait("tensor", "SD", c_wq)
                            for hh in range(4):
                                h = hg * 4 + hh
                                ps = bS[h % 2]
                                for fc in range(NF):
                                    rec.op("tensor", lambda t, fc=fc, hh=hh, ps=ps, bi=bi: t.matmul(
                                        ps, slab3[:, fc, hh * 128:(hh + 1) * 128],
                                        XN3[:, fc, bi:bi + 512],
                                        start=(fc == 0), stop=(fc == NF - 1)))
                                ct = rec.bump("tensor")
                                rec.wait("vector", "T", ct)
                                cv = rec.op("vector", lambda v, ps=ps: v.tensor_copy(qpre[:, :], ps), inc="V")
                                rec.wait("tensor", "V", cv)
                                ct2 = rec.op("tensor", lambda t: t.matmul(
                                    bR, rot_sb[:, :], qpre[:, :], start=True, stop=True), inc="T")
                                rec.wait("vector", "T", ct2)
                                rec.op("vector", lambda v: v.tensor_tensor(
                                    t2s[:, :], bR, sin_s[:, :], ALU.mult))
                                rec.op("vector", lambda v: v.tensor_tensor(
                                    t3s[:, :], qpre[:, :], cos_s[:, :], ALU.mult))
                                cv2 = rec.op("vector", lambda v, h=h: v.tensor_tensor(
                                    Qarea[:, h * 512:(h + 1) * 512],
                                    t3s[:, :], t2s[:, :], ALU.add), inc="V")
                                rec.wait("tensor", "V", cv2)
                            rec.wait("sync", "V", rec.cnt["V"])
                        njc = min((bi + 512 + 128) // 128, 16)
                        for h in range(H1):
                            kh = h // 4
                            rec.wait("vector", "A", rec.cnt["A"])
                            cq = rec.op("vector", lambda v, h=h: v.tensor_tensor(
                                t2s[:, :], Qarea[:, h * 512:(h + 1) * 512],
                                Qarea[:, h * 512:(h + 1) * 512], ALU.mult), inc="V")
                            rec.wait("tensor", "V", cq)
                            ct = rec.op("tensor", lambda t: t.matmul(
                                bQ, ones_128x1_f32[:, :], t2s[:, :], start=True, stop=True), inc="T")
                            rec.wait("vector", "T", ct)
                            rec.op("vector", lambda v: v.reduce_max(csc[:, 1:2], bQ, axis=AX.X), inc="V")
                            rec.op("vector", lambda v, kh=kh: v.tensor_tensor(
                                csc[:, 2:3], csc[:, 1:2], kn2m[:, kh:kh + 1], ALU.mult), inc="V")
                            cvb = rec.op("vector", lambda v: v.tensor_scalar_mul(
                                csc[:, 3:4], csc[:, 2:3], 1.0 / 128.0), inc="V")
                            rec.wait("scalar", "V", cvb)
                            rec.op("scalar", lambda s: s.activation(
                                csc[:, 4:5], csc[:, 3:4], AF.Ln), inc="A")
                            caa = rec.op("scalar", lambda s: s.activation(
                                csc[:, 5:6], csc[:, 4:5], AF.Exp, scale=0.5), inc="A")
                            rec.wait("vector", "A", caa)
                            cvn = rec.op("vector", lambda v: v.tensor_scalar_mul(
                                csc[:, 6:7], csc[:, 5:6], -1.0), inc="V")
                            rec.wait("tensor", "V", cvn)
                            ctn = rec.op("tensor", lambda t: t.matmul(
                                bN, ones_1x128_f32[:, :], csc[:, 6:7], start=True, stop=True), inc="T")
                            rec.wait("vector", "T", ctn)
                            cvn2 = rec.op("vector", lambda v: v.tensor_copy(negc_col[:, :], bN), inc="V")
                            rec.wait("scalar", "V", cvn2)
                            pexp = {}
                            for jc in range(njc):
                                pp = jc % 2
                                psS = bS[pp]
                                if jc >= 2:
                                    rec.wait("tensor", "A", pexp[jc - 2])
                                rec.op("tensor", lambda t, jc=jc, h=h, kh=kh, psS=psS: t.matmul(
                                    psS, Karea[:, kh * 2048 + jc * 128: kh * 2048 + (jc + 1) * 128],
                                    Qarea[:, h * 512:(h + 1) * 512], start=True, stop=True))
                                cts = rec.bump("tensor")
                                rec.wait("scalar", "T", cts)
                                pexp[jc] = rec.op("scalar", lambda s, jc=jc, pp=pp, psS=psS: s.activation(
                                    p_sb[:, pp * 512:(pp + 1) * 512], psS, AF.Exp,
                                    bias=negc_col[:, :], scale=SCALE), inc="A")
                                delta = jc * 128 - bi
                                if delta >= 128:
                                    di = delta // 128 - 1
                                    rec.wait("vector", "A", pexp[jc])
                                    cvmk = rec.op("vector", lambda v, pp=pp, di=di: v.tensor_tensor(
                                        p_sb[:, pp * 512:(pp + 1) * 512],
                                        p_sb[:, pp * 512:(pp + 1) * 512],
                                        masks_sb[:, di * 512:(di + 1) * 512], ALU.mult), inc="V")
                                    rec.wait("tensor", "V", cvmk)
                                else:
                                    rec.wait("tensor", "A", pexp[jc])
                                rec.op("tensor", lambda t, jc=jc, pp=pp, njc=njc: t.matmul(
                                    bL, ones_128x1_bf[:, :], p_sb[:, pp * 512:(pp + 1) * 512],
                                    start=(jc == 0), stop=(jc == njc - 1)))
                                rec.op("tensor", lambda t, jc=jc, pp=pp, kh=kh, njc=njc: t.matmul(
                                    bO, Varea[:, kh * 2048 + jc * 128: kh * 2048 + (jc + 1) * 128],
                                    p_sb[:, pp * 512:(pp + 1) * 512],
                                    start=(jc == 0), stop=(jc == njc - 1)))
                                rec.bump("tensor")
                            ctl = rec.cnt["T"]
                            rec.wait("vector", "T", ctl)
                            cvr = rec.op("vector", lambda v: v.reciprocal(r_row[:, :], bL), inc="V")
                            rec.wait("tensor", "V", cvr)
                            ctr = rec.op("tensor", lambda t: t.matmul(
                                bR, ones_1x128_f32[:, :], r_row[:, :], start=True, stop=True), inc="T")
                            rec.wait("vector", "T", ctr)
                            rec.op("vector", lambda v: v.tensor_copy(rb_sb[:, :], bR), inc="V")
                            cvo = rec.op("vector", lambda v, h=h: v.tensor_tensor(
                                Tarea[:, h * 512:(h + 1) * 512], bO, rb_sb[:, :], ALU.mult), inc="V")
                            rec.wait("tensor", "V", cvo)
                        for tt in range(4):
                            ti = ib * 4 + tt
                            c_ld = rec.dma("sync", x_tile[:, :], X[ti * 128:(ti + 1) * 128, :])
                            rec.wait("vector", "SD", c_ld)
                            evs = []
                            for fo in range(4):
                                ps = bY[fo % 2]
                                for fc in range(NF):
                                    rec.dma("sync", slab3[:, fc, :],
                                            WO_BF[fo * 512:(fo + 1) * 512, fc * 128:(fc + 1) * 128],
                                            transpose=True)
                                c_wo = rec.cnt["SD"]
                                rec.wait("tensor", "SD", c_wo)
                                if len(evs) >= 2:
                                    rec.wait("tensor", "V", evs[-2])
                                for fc in range(NF):
                                    rec.op("tensor", lambda t, fc=fc, tt=tt, fo=fo, ps=ps: t.matmul(
                                        ps, Tarea[:, fc * 512 + tt * 128: fc * 512 + tt * 128 + 128],
                                        slab3[:, fc, :],
                                        start=(fc == 0), stop=(fc == NF - 1)))
                                ct = rec.bump("tensor")
                                rec.wait("vector", "T", ct)
                                cv = rec.op("vector", lambda v, fo=fo, ps=ps: v.tensor_tensor(
                                    x_new[:, fo * 512:(fo + 1) * 512],
                                    x_tile[:, fo * 512:(fo + 1) * 512],
                                    ps, ALU.add), inc="V")
                                evs.append(cv)
                                rec.wait("sync", "V", cv)
                            rec.wait("gpsimd", "V", evs[-1])
                            c_st = rec.dma("gpsimd", X[ti * 128:(ti + 1) * 128, :], x_new[:, :])
                            rec.wait("vector", "GD", c_st)
                            rec.wait("sync", "V", evs[-1])
                        rec.barrier()
                    rec.barrier()

            rec.barrier()

        if upto >= 2:
            block_L0_ffn()

        def block_L1_ffn():
            wg, wu, wd = inp["L1_ffn_w_gate"], inp["L1_ffn_w_up"], inp["L1_ffn_w_down"]
            sg, su, sd = inp["L1_ffn_s_gate"], inp["L1_ffn_s_up"], inp["L1_ffn_s_down"]
            with ExitStack() as bctx:
                sbl = lambda name, shape, dt: bctx.enter_context(nc.sbuf_tensor(name, shape, dt))
                wg_slab = sbl("wg_slab", [128, 2 * 2048], BF16)
                wu_slab = sbl("wu_slab", [128, 2 * 2048], BF16)
                wd_slab = sbl("wd_slab", [128, 2 * 4 * 512], BF16)
                sg_col = sbl("sg_col", [128, 64], F32)
                nsg_col = sbl("nsg_col", [128, 64], F32)
                su_col = sbl("su_col", [128, 64], F32)
                e_scr = sbl("e_scr", [128, 512], F32)
                sig_scr = sbl("sig_scr", [128, 512], F32)
                gate_scr = sbl("gate_scr", [128, 512], F32)
                y_scr = sbl("y_scr", [128, 512], F32)

                c1 = rec.dma("sync", sg_col[:, :], sg.rearrange("(c p) -> p c", p=128),
                             allow_slow_non_contiguous=True)
                c2 = rec.dma("sync", su_col[:, :], su.rearrange("(c p) -> p c", p=128),
                             allow_slow_non_contiguous=True)
                c3 = rec.dma("sync", brow[0:1, 0:2048], sd.rearrange("(o c) -> o c", o=1))
                rec.wait("vector", "SD", max(c1, c2))
                rec.op("vector", lambda v: v.tensor_scalar_mul(nsg_col[:, :], sg_col[:, :], -1.0),
                       inc="V")
                rec.barrier()

                ln_pass(X)

                with nc.psum_tensor("sd_ps", [128, 2048], F32) as sd_ps:
                    rec.wait("tensor", "SD", c3)
                    for j in range(4):
                        rec.op("tensor", lambda t, j=j: t.matmul(
                            sd_ps[:, j * 512:(j + 1) * 512], ones_1x128_f32[:, :],
                            brow[0:1, j * 512:(j + 1) * 512], start=True, stop=True))
                    ct = rec.bump("tensor")
                    rec.wait("vector", "T", ct)
                    rec.op("vector", lambda v: v.tensor_copy(bc_f32[:, :], sd_ps[:, :]), inc="V")
                rec.barrier()

                fuse3 = A2.ap().rearrange("p (ic t) -> p ic t", t=512)
                wg3 = wg_slab.ap().rearrange("p (b fc o) -> p b fc o", b=2, o=128)
                wu3 = wu_slab.ap().rearrange("p (b fc o) -> p b fc o", b=2, o=128)

                with nc.psum_tensor("f_ps", [128, 4096], F32) as FPS:
                    bG = (FPS[:, 0:512], FPS[:, 512:1024])
                    bU = (FPS[:, 1024:1536], FPS[:, 1536:2048])
                    bY = [FPS[:, (4 + t) * 512:(5 + t) * 512] for t in range(4)]
                    for g in range(4):
                        ldg = {}
                        cts_g = {}
                        for oc in range(64):
                            pb = oc % 2
                            if oc == 0:
                                rec.dma("gpsimd", wg3[:, pb, :, :],
                                        wg[:, 0:128].rearrange("(fc p) o -> p fc o", p=128))
                                ldg[0] = rec.dma("gpsimd", wu3[:, pb, :, :],
                                                 wu[:, 0:128].rearrange("(fc p) o -> p fc o", p=128))
                            rec.wait("tensor", "GD", ldg[oc])
                            if oc + 1 < 64:
                                npb = (oc + 1) % 2
                                if oc >= 1:
                                    rec.wait("gpsimd", "T", cts_g[oc - 1])
                                rec.dma("gpsimd", wg3[:, npb, :, :],
                                        wg[:, (oc + 1) * 128:(oc + 2) * 128]
                                        .rearrange("(fc p) o -> p fc o", p=128))
                                ldg[oc + 1] = rec.dma("gpsimd", wu3[:, npb, :, :],
                                                      wu[:, (oc + 1) * 128:(oc + 2) * 128]
                                                      .rearrange("(fc p) o -> p fc o", p=128))
                            psg = bG[pb]
                            psu = bU[pb]
                            for fc in range(NF):
                                rec.op("tensor", lambda t, fc=fc, g=g, pb=pb, psg=psg: t.matmul(
                                    psg, wg3[:, pb, fc, :], XN3[:, fc, g * 512:(g + 1) * 512],
                                    start=(fc == 0), stop=(fc == NF - 1)))
                            for fc in range(NF):
                                rec.op("tensor", lambda t, fc=fc, g=g, pb=pb, psu=psu: t.matmul(
                                    psu, wu3[:, pb, fc, :], XN3[:, fc, g * 512:(g + 1) * 512],
                                    start=(fc == 0), stop=(fc == NF - 1)))
                            ct = rec.bump("tensor")
                            cts_g[oc] = ct
                            rec.wait("scalar", "T", ct)
                            cae = rec.op("scalar", lambda s, oc=oc, psg=psg: s.activation(
                                e_scr[:, :], psg, AF.Exp, scale=nsg_col[:, oc:oc + 1]), inc="A")
                            rec.wait("vector", "A", cae)
                            rec.op("vector", lambda v: v.tensor_scalar_add(sig_scr[:, :], e_scr[:, :], 1.0))
                            rec.op("vector", lambda v: v.reciprocal(sig_scr[:, :], sig_scr[:, :]))
                            rec.op("vector", lambda v, oc=oc, psg=psg: v.scalar_tensor_tensor(
                                gate_scr[:, :], psg, sg_col[:, oc:oc + 1], sig_scr[:, :],
                                ALU.mult, ALU.mult))
                            cvf = rec.op("vector", lambda v, oc=oc, psu=psu: v.scalar_tensor_tensor(
                                fuse3[:, oc, :], psu, su_col[:, oc:oc + 1], gate_scr[:, :],
                                ALU.mult, ALU.mult), inc="V")
                            rec.wait("tensor", "V", cvf)
                        lds = {}
                        cts_d = {}
                        evd = []
                        wd4 = wd_slab.ap().rearrange("p (b c d) -> p b c d", b=2, d=512)
                        for fo in range(4):
                            for icg in range(16):
                                pb = icg % 2
                                if icg == 0:
                                    lds[0] = rec.dma("gpsimd", wd4[:, 0, :, :],
                                                     wd[0:512, fo * 512:(fo + 1) * 512]
                                                     .rearrange("(c p) d -> p c d", p=128))
                                rec.wait("tensor", "GD", lds[icg])
                                if icg + 1 < 16:
                                    npb = (icg + 1) % 2
                                    if icg >= 1:
                                        rec.wait("gpsimd", "T", cts_d[icg - 1])
                                    lds[icg + 1] = rec.dma(
                                        "gpsimd", wd4[:, npb, :, :],
                                        wd[(icg + 1) * 512:(icg + 2) * 512, fo * 512:(fo + 1) * 512]
                                        .rearrange("(c p) d -> p c d", p=128))
                                for c4 in range(4):
                                    ic = icg * 4 + c4
                                    for tt in range(4):
                                        rec.op("tensor", lambda t, ic=ic, tt=tt, pb=pb, c4=c4: t.matmul(
                                            bY[tt], fuse3[:, ic, tt * 128:(tt + 1) * 128],
                                            wd4[:, pb, c4, :],
                                            start=(ic == 0), stop=(ic == 63)))
                                cts_d[icg] = rec.bump("tensor")
                            ct = rec.cnt["T"]
                            rec.wait("vector", "T", ct)
                            for tt in range(4):
                                ti = g * 4 + tt
                                c_ld = rec.dma("sync", x_tile[0:128, 0:512],
                                               X[ti * 128:(ti + 1) * 128, fo * 512:(fo + 1) * 512])
                                rec.wait("vector", "SD", c_ld)
                                rec.op("vector", lambda v, tt=tt, fo=fo: v.tensor_tensor(
                                    y_scr[:, :], bY[tt], bc_f32[:, fo * 512:(fo + 1) * 512],
                                    ALU.mult))
                                cv = rec.op("vector", lambda v: v.tensor_tensor(
                                    x_new[:, 0:512], x_tile[0:128, 0:512],
                                    y_scr[:, :], ALU.add), inc="V")
                                rec.wait("gpsimd", "V", cv)
                                c_st = rec.dma("gpsimd",
                                               out_ap[ti * 128:(ti + 1) * 128, fo * 512:(fo + 1) * 512],
                                               x_new[:, 0:512])
                                rec.wait("vector", "GD", c_st)
                                rec.wait("sync", "V", cv)
                            rec.wait("tensor", "V", rec.cnt["V"])
                        rec.barrier()
                rec.barrier()


        if upto >= 3:
            block_L1_attn()
        if upto >= 4:
            block_L1_ffn()

        if not KDUMP and upto < 4:
            cfin = rec.dma("gpsimd", out_ap, X)
            rec.wait("gpsimd", "GD", cfin)
        rec.barrier()

        with nc.Block() as block:
            rec.replay(block)

    return nc



def make_in_maps(inputs, upto=4):
    consts_np = build_consts()
    x = np.asarray(inputs["L0_attn_x"])
    per_core = []
    base = {}
    for k, v in inputs.items():
        if k == "L0_attn_x":
            continue
        base[k] = np.ascontiguousarray(np.asarray(v))
    for name, arr in consts_np.items():
        base[name] = np.ascontiguousarray(arr)
    for c in range(N_CORES):
        m = dict(base)
        m["x_in"] = np.ascontiguousarray(x[c])
        per_core.append(m)
    return per_core


def kernel(**inputs):
    nc = build_program(upto=4)
    in_maps = make_in_maps(inputs)
    res = run_bass_kernel_spmd(nc, in_maps, core_ids=list(range(N_CORES)))
    return np.stack([np.asarray(res.results[c]["out"]) for c in range(N_CORES)], axis=0)
